# version 23
# baseline (speedup 1.0000x reference)
import numpy as np
from contextlib import ExitStack

import concourse.bass as bass
import concourse.mybir as mybir
from concourse.bass_utils import run_bass_kernel_spmd

F32 = mybir.dt.float32
BF16 = mybir.dt.bfloat16
AF = mybir.ActivationFunctionType
OP = mybir.AluOpType

D = 1024
DH = 4096
S_LEN = 2048
P_LEN = 6144
BC = 128
TILE = 512
N_TILE = S_LEN // TILE
BPT = TILE // BC
NPT = P_LEN // BC
PGRP = 8
NC_D = D // 128
NC_H = DH // 128


class Ctr:
    def __init__(self):
        self.v = 0

    def add(self, n):
        self.v += n
        return self.v


def build_nc():
    nc = bass.Bass("TRN2", target_bir_lowering=False, detect_race_conditions=False)

    xs = nc.dram_tensor("xs", [S_LEN, D], F32, kind="ExternalInput")
    xp = nc.dram_tensor("xp", [P_LEN, D], F32, kind="ExternalInput")
    wwt = nc.dram_tensor("wwt", [D, D], F32, kind="ExternalInput")
    wrt = nc.dram_tensor("wrt", [D, D], F32, kind="ExternalInput")
    w12 = nc.dram_tensor("w12", [NC_H, 128, 2 * NC_D * 128], F32, kind="ExternalInput")
    h1b = nc.dram_tensor("h1b", [NC_H, 128, NC_H * 128], F32, kind="ExternalInput")
    h2b = nc.dram_tensor("h2b", [NC_H, 128, NC_H * 128], F32, kind="ExternalInput")
    w3r = nc.dram_tensor("w3r", [NC_H // 2, 128, 2 * D], F32, kind="ExternalInput")
    w1s_d = nc.dram_tensor("w1s", [128, NC_H], F32, kind="ExternalInput")
    w2s_d = nc.dram_tensor("w2s", [128, NC_H], F32, kind="ExternalInput")
    h1s_d = nc.dram_tensor("h1s", [128, NC_H], F32, kind="ExternalInput")
    h2s_d = nc.dram_tensor("h2s", [128, NC_H], F32, kind="ExternalInput")
    s3bc_d = nc.dram_tensor("s3bc", [128, D], F32, kind="ExternalInput")
    dec = nc.dram_tensor("dec", [128, 1], F32, kind="ExternalInput")
    la = nc.dram_tensor("la", [128, 1], F32, kind="ExternalInput")
    o_seg = nc.dram_tensor("o_seg", [S_LEN, D], F32, kind="ExternalOutput")
    sc12 = nc.dram_tensor("sc12", [NC_H, 128, 2 * NC_D * 128], BF16)
    sch1 = nc.dram_tensor("sch1", [NC_H, 128, NC_H * 128], BF16)
    sch2 = nc.dram_tensor("sch2", [NC_H, 128, NC_H * 128], BF16)
    sc3 = nc.dram_tensor("sc3", [NC_H // 2, 128, 2 * D], BF16)

    es = ExitStack()
    sb = lambda name, shape, dt: es.enter_context(nc.sbuf_tensor(name, shape, dt))
    psum = lambda name, shape: es.enter_context(nc.psum_tensor(name, shape, F32))
    sem = lambda name: es.enter_context(nc.semaphore(name))

    dX = sem("dX")
    dXP0 = sem("dXP0")
    dXP1 = sem("dXP1")
    dXP2 = sem("dXP2")
    dXP3 = sem("dXP3")
    dO = sem("dO")
    dC = sem("dC")
    dWR = sem("dWR")
    dW0 = sem("dW0")
    dW1 = sem("dW1")
    dF0 = sem("dF0")
    dF1 = sem("dF1")
    dSC0 = sem("dSC0")
    dSC1 = sem("dSC1")
    dK = sem("dK")
    dK0 = sem("dK0")
    dK1 = sem("dK1")
    dK2 = sem("dK2")
    dK3 = sem("dK3")
    tS = sem("tS")
    vS = sem("vS")
    sS = sem("sS")
    gS = sem("gS")
    c_dX, c_dO, c_dC = Ctr(), Ctr(), Ctr()
    dXPs = [dXP0, dXP1, dXP2, dXP3]
    c_dXPs = [Ctr() for _ in range(4)]
    c_dW0, c_dW1, c_dK, c_dWR = Ctr(), Ctr(), Ctr(), Ctr()
    dKs = [dK0, dK1, dK2, dK3]
    c_dKs = [Ctr() for _ in range(4)]
    dFs = [dF0, dF1]
    c_dFs = [Ctr(), Ctr()]
    dSCs = [dSC0, dSC1]
    c_dSCs = [Ctr(), Ctr()]
    c_t, c_v, c_s, c_g = Ctr(), Ctr(), Ctr(), Ctr()
    dWs = [dW0, dW1]
    c_dWs = [c_dW0, c_dW1]

    wwt_sb = sb("wwt_sb", [128, NC_D * D], BF16)
    wrt_sb = sb("wrt_sb", [128, NC_D * D], BF16)
    wstate = sb("wstate", [128, NC_D * D], BF16)
    s3bc = sb("s3bc_sb", [128, D], F32)
    scl = {nm: sb(nm + "_sb", [128, NC_H], F32)
           for nm in ("w1s", "w2s", "h1s", "h2s")}
    dec_sb = sb("dec_sb", [128, 1], F32)
    la_sb = sb("la_sb", [128, 1], F32)
    gam = sb("gam", [128, 1], F32)
    lg = sb("lg", [128, 1], F32)
    alpha = sb("alpha", [128, 1], F32)
    gw_col = sb("gw_col", [128, 1], F32)
    gc_col = sb("gc_col", [128, 1], F32)
    eps_c = sb("eps_c", [128, 1], F32)
    nh_c = sb("nh_c", [128, 1], F32)
    i127 = sb("i127", [128, 1], F32)
    c128 = sb("c128", [128, 1], F32)
    kcol = sb("kcol", [128, 1], F32)
    icf = sb("icf", [128, 128], F32)
    identb = sb("identb", [128, 128], BF16)
    maskT = sb("maskT", [128, 128], F32)
    gp_bc = sb("gp_bc", [128, 128], BF16)
    j48 = sb("j48", [128, NPT], F32)
    e48 = sb("e48", [128, NPT], F32)
    gpref = sb("gpref", [128, NPT], F32)
    zrow = sb("zrow", [1, D], BF16)
    bnd_bf = sb("bnd_bf", [1, D], BF16)
    srow = sb("srow", [128, 2], F32)
    qrow = sb("qrow", [128, 2], F32)
    mcol = sb("mcol", [128, 2], F32)
    msq = sb("msq", [128, 2], F32)
    varc = sb("varc", [128, 2], F32)
    rstd = sb("rstd", [128, 2], F32)
    junk = sb("junk", [128, D], BF16)

    ARENA_B = 114688
    parena = sb("parena", [128, ARENA_B], mybir.dt.uint8)

    def overlay(name, shape, dtype, off):
        h = nc.virtual_sbuf_tensor(name, shape, dtype)
        nc._tensor(h.name, shape, dtype, type="SB")
        mloc = nc.lookup_mloc(h)
        mloc.allocated = not nc.target_bir_lowering
        mloc.addr = nc.lookup_mloc(parena).addr + off
        return h

    p01 = psum("p01", [128, 1024])
    p2 = psum("p2", [128, 1024])
    p3 = psum("p3", [128, 1024])
    p4 = psum("p4", [128, 1024])

    def run_block(plan):
        with nc.Block() as blk:
            for eng in ("sync", "gpsimd", "scalar", "vector", "tensor"):
                fns = plan.get(eng)
                if fns:
                    getattr(blk, eng)(lambda e, fns=fns: [f(e) for f in fns])

    def mk_plan():
        return {k: [] for k in ("sync", "gpsimd", "scalar", "vector", "tensor")}

    class LNJob:
        def __init__(self, xap, out_ap, par, pre_s=(), pre_v=()):
            self.xap, self.oap, self.par = xap, out_ap, par
            self.pre_s, self.pre_v = tuple(pre_s), tuple(pre_v)

        def emit_s1(self, plan):
            xap, w, p = self.xap, self.pre_s, self.par

            def s1(e):
                for sm, val in w:
                    e.wait_ge(sm, val)
                e.activation(junk[:, :], xap, AF.Copy,
                             accum_out=srow[:, p:p + 1])
                e.activation(junk[:, :], xap, AF.Square,
                             accum_out=qrow[:, p:p + 1]).then_inc(sS, 1)
            plan["scalar"].append(s1)
            self.s_done = c_s.add(1)

        def emit_v1(self, plan):
            w, sd, p = self.pre_v, self.s_done, self.par
            vself = c_v.v + 1

            def v1(e):
                for sm, val in w:
                    e.wait_ge(sm, val)
                e.wait_ge(sS, sd)
                e.tensor_scalar(msq[:, p:p + 1], srow[:, p:p + 1],
                                srow[:, p:p + 1], 1.0 / (D * D),
                                OP.mult, OP.mult)
                e.tensor_scalar(mcol[:, p:p + 1], srow[:, p:p + 1], 1.0 / D,
                                None, OP.mult).then_inc(vS, 1)
                e.wait_ge(vS, vself)
                e.scalar_tensor_tensor(varc[:, p:p + 1], qrow[:, p:p + 1],
                                       1.0 / D, msq[:, p:p + 1],
                                       OP.mult, OP.subtract).then_inc(vS, 1)
            plan["vector"].append(v1)
            c_v.add(1)
            self.v_done = c_v.add(1)

        def emit_s2(self, plan):
            vd, p = self.v_done, self.par
            sself = c_s.v + 1

            def s2(e):
                e.wait_ge(vS, vd)
                e.activation(varc[:, p:p + 1], varc[:, p:p + 1], AF.Ln,
                             bias=eps_c[:, 0:1]).then_inc(sS, 1)
                e.wait_ge(sS, sself)
                e.activation(rstd[:, p:p + 1], varc[:, p:p + 1], AF.Exp,
                             scale=nh_c[:, 0:1]).then_inc(sS, 1)
            plan["scalar"].append(s2)
            c_s.add(1)
            self.s2_done = c_s.add(1)

        def emit_v2(self, plan):
            xap, oap, sd, p = self.xap, self.oap, self.s2_done, self.par
            ev = getattr(self, 'extra_v2', None)

            def v2(e, ev=ev):
                e.wait_ge(sS, sd)
                if ev:
                    e.wait_ge(ev[0], ev[1])
                e.tensor_scalar(oap, xap, mcol[:, p:p + 1], rstd[:, p:p + 1],
                                OP.subtract, OP.mult).then_inc(vS, 1)
            plan["vector"].append(v2)
            self.ln_ready = c_v.add(1)
            return self.ln_ready

    def emit_ln(plan, xap, out_ap, pre_waits_scalar=(), pre_waits_vector=(),
                par=0):
        job = LNJob(xap, out_ap, par, pre_waits_scalar, pre_waits_vector)
        job.emit_s1(plan)
        job.emit_v1(plan)
        job.emit_s2(plan)
        return job.emit_v2(plan)

    setup_tmp = ExitStack()
    dm1 = setup_tmp.enter_context(nc.sbuf_tensor("dm1", [128, 128], F32))
    dcl = setup_tmp.enter_context(nc.sbuf_tensor("dcl", [128, 128], F32))
    caus = setup_tmp.enter_context(nc.sbuf_tensor("caus", [128, 128], F32))
    expd = setup_tmp.enter_context(nc.sbuf_tensor("expd", [128, 128], F32))
    plan = mk_plan()

    def su_sync(e):
        e.dma_start(dec_sb[:, :], dec.ap()).then_inc(dC, 16)
        e.dma_start(la_sb[:, :], la.ap()).then_inc(dC, 16)
        for sbuf, dr in ((scl["w1s"], w1s_d), (scl["w2s"], w2s_d),
                         (scl["h1s"], h1s_d), (scl["h2s"], h2s_d)):
            e.dma_start(sbuf[:, :], dr.ap()).then_inc(dC, 16)
        e.dma_start(s3bc[:, :], s3bc_d.ap()).then_inc(dC, 16)
    plan["sync"].append(su_sync)
    dc_base = c_dC.add(16 * 7)

    def su_gp(e):
        e.iota(icf[:, :], [[1, 128]], channel_multiplier=0,
               allow_small_or_imprecise_dtypes=True)
        e.iota(kcol[:, 0:1], [[0, 1]], channel_multiplier=1,
               allow_small_or_imprecise_dtypes=True)
        e.iota(j48[:, :], [[128, NPT]], channel_multiplier=1,
               allow_small_or_imprecise_dtypes=True)
        e.memset(c128[:, 0:1], float(BC))
        e.memset(eps_c[:, 0:1], 1e-5)
        e.memset(nh_c[:, 0:1], -0.5)
        e.memset(zrow[0:1, :], 0.0).then_inc(gS, 1)
        for jc in range(NC_D):
            e.dma_start(wwt_sb[:, jc * D:(jc + 1) * D],
                        wwt.ap()[jc * 128:(jc + 1) * 128, :]).then_inc(dW0, 16)
    plan["gpsimd"].append(su_gp)
    g_setup = c_g.add(1)
    dw_wwt = c_dW0.add(16 * NC_D)

    def su_scalar(e):
        e.wait_ge(dC, dc_base)
        e.wait_ge(gS, g_setup)
        e.activation(gam[:, 0:1], dec_sb[:, 0:1], AF.Sigmoid)
        e.activation(alpha[:, 0:1], la_sb[:, 0:1], AF.Exp).then_inc(sS, 1)
    plan["scalar"].append(su_scalar)
    s_ga = c_s.add(1)

    def su_vec1(e, sg=s_ga):
        e.wait_ge(sS, sg)
        e.tensor_scalar_max(gam[:, 0:1], gam[:, 0:1], 1e-30)
        e.tensor_scalar(i127[:, 0:1], kcol[:, 0:1], -1.0, 127.0, OP.mult, OP.add)
        e.tensor_scalar(dm1[:, :], icf[:, :], kcol[:, 0:1], 1.0,
                        OP.subtract, OP.subtract)
        e.tensor_scalar_max(dcl[:, :], dm1[:, :], 0.0)
        e.tensor_scalar(caus[:, :], dm1[:, :], 0.0, None, OP.is_ge)
        e.tensor_scalar(e48[:, :], j48[:, :], -1.0, float(P_LEN - 1),
                        OP.mult, OP.add)
        e.tensor_scalar(identb[:, :], icf[:, :], kcol[:, 0:1], None,
                        OP.is_equal).then_inc(vS, 1)
    plan["vector"].append(su_vec1)
    v_pre = c_v.add(1)

    def su_scalar3(e, vp=v_pre, ss=c_s.v + 1):
        e.wait_ge(vS, vp)
        e.activation(lg[:, 0:1], gam[:, 0:1], AF.Ln).then_inc(sS, 1)
        e.wait_ge(sS, ss)
        e.activation(gw_col[:, 0:1], i127[:, 0:1], AF.Exp, scale=lg[:, 0:1])
        e.activation(gc_col[:, 0:1], c128[:, 0:1], AF.Exp, scale=lg[:, 0:1])
        e.activation(expd[:, :], dcl[:, :], AF.Exp, scale=lg[:, 0:1])
        e.activation(gpref[:, :], e48[:, :], AF.Exp, scale=lg[:, 0:1])
        e.activation(gp_bc[:, :], icf[:, :], AF.Exp,
                     scale=lg[:, 0:1]).then_inc(sS, 1)
    plan["scalar"].append(su_scalar3)
    c_s.add(1)
    s_exp = c_s.add(1)

    def su_vec2(e, se=s_exp):
        e.wait_ge(sS, se)
        e.tensor_mul(maskT[:, :], expd[:, :], caus[:, :]).then_inc(vS, 1)
    plan["vector"].append(su_vec2)
    c_v.add(1)

    def su_wr_load(e):
        for ic in range(NC_D):
            e.dma_start(wrt_sb[:, ic * D:(ic + 1) * D],
                        wrt.ap()[ic * 128:(ic + 1) * 128, :]).then_inc(dW0, 16)
    plan["gpsimd"].append(su_wr_load)
    dw_wrt = c_dW0.add(16 * NC_D)

    def su_wr_scale(e, dcn=dw_wrt, sg=s_ga):
        e.wait_ge(dW0, dcn)
        e.wait_ge(sS, sg)
        e.tensor_scalar_mul(wrt_sb[:, :], wrt_sb[:, :],
                            alpha[:, 0:1]).then_inc(vS, 1)
    plan["vector"].append(su_wr_scale)
    c_v.add(1)

    run_block(plan)
    setup_tmp.close()

    if True:
        xpbuf = overlay("xpbuf", [128, 8 * D], F32, 0)
        lnpb = overlay("lnpb", [128, 8 * D], BF16, 32768)
        glnp = overlay("glnp", [128, PGRP * D], BF16, 49152)
        wkp = overlay("wkp", [128, PGRP * D], BF16, 65536)
        bsb = overlay("bsb", [128, NC_D * D], F32, 81920)
        plan = mk_plan()
        n_grp = NPT // PGRP
        ln_ready = [0] * NPT
        gl_ready = [0] * NPT
        wk_ready = [0] * NPT
        grp_mm_done = [0] * n_grp
        badd_ctr = {}
        pp = [p2, p3]
        jobs = [None] * NPT

        for s in range(NPT + 1):
            if s >= 1:
                q = s - 1
                par8 = q % 8
                par4 = q % 4
                slot = q % PGRP
                grp = q // PGRP
                jobs[q].emit_s2(plan)
                if q >= 7:
                    jobs[q].extra_v2 = (dKs[(q - 7) % 4], wk_ready[q - 7])
                ln_ready[q] = jobs[q].emit_v2(plan)

                def pf_gl(e, q=q, par8=par8, slot=slot,
                          gm=(grp_mm_done[grp - 1] if grp >= 1 else 0)):
                    if gm:
                        e.wait_ge(tS, gm)
                    e.tensor_scalar_mul(glnp[:, slot * D:(slot + 1) * D],
                                        lnpb[:, par8 * D:(par8 + 1) * D],
                                        gpref[:, q:q + 1]).then_inc(vS, 1)
                plan["vector"].append(pf_gl)
                gl_ready[q] = c_v.add(1)

                dk, c_dk = dKs[par4], c_dKs[par4]

                def pf_wk(e, q=q, par8=par8, slot=slot, lr=ln_ready[q], dk=dk,
                          gm=(grp_mm_done[grp - 1] if grp >= 1 else 0)):
                    e.wait_ge(vS, lr)
                    if gm:
                        e.wait_ge(tS, gm)
                    e.dma_start(wkp[1:128, slot * D:(slot + 1) * D],
                                lnpb[0:127, par8 * D:(par8 + 1) * D]
                                ).then_inc(dk, 16)
                    if q == 0:
                        e.dma_start(wkp[0:1, slot * D:(slot + 1) * D],
                                    zrow[0:1, :]).then_inc(dk, 16)
                    else:
                        e.dma_start(wkp[0:1, slot * D:(slot + 1) * D],
                                    lnpb[127:128, ((q - 1) % 8) * D:
                                         ((q - 1) % 8 + 1) * D]).then_inc(dk, 16)
                plan["gpsimd"].append(pf_wk)
                wk_ready[q] = c_dk.add(32)

                if slot == PGRP - 1:
                    wk_counts = [c_dKs[i].v for i in range(4)]

                    def pf_mm(e, grp=grp, glr=gl_ready[q], wkc=tuple(wk_counts)):
                        e.wait_ge(vS, glr)
                        for i in range(4):
                            e.wait_ge(dKs[i], wkc[i])
                        for jc in range(NC_D):
                            pb = pp[jc % 2]
                            if jc >= 2:
                                e.wait_ge(vS, badd_ctr[grp][jc - 2])
                            elif grp >= 1:
                                e.wait_ge(vS, badd_ctr[grp - 1][NC_D + jc - 2])
                            last = None
                            for t8 in range(PGRP):
                                for hf in range(2):
                                    last = e.matmul(
                                        pb[:, hf * 512:(hf + 1) * 512],
                                        glnp[:, t8 * D + jc * 128:
                                             t8 * D + (jc + 1) * 128],
                                        wkp[:, t8 * D + hf * 512:
                                            t8 * D + (hf + 1) * 512],
                                        start=(t8 == 0), stop=(t8 == PGRP - 1))
                            last.then_inc(tS, 1)
                    plan["tensor"].append(pf_mm)
                    mm_base = c_t.v
                    c_t.add(NC_D)
                    grp_mm_done[grp] = c_t.v

                    def pf_badd(e, grp=grp, mb=mm_base):
                        for jc in range(NC_D):
                            pb = pp[jc % 2]
                            e.wait_ge(tS, mb + jc + 1)
                            if grp == 0:
                                e.tensor_copy(bsb[:, jc * D:(jc + 1) * D],
                                              pb[:, :]).then_inc(vS, 1)
                            else:
                                e.tensor_add(bsb[:, jc * D:(jc + 1) * D],
                                             bsb[:, jc * D:(jc + 1) * D],
                                             pb[:, :]).then_inc(vS, 1)
                    plan["vector"].append(pf_badd)
                    badd_ctr[grp] = [c_v.add(1) for _ in range(NC_D)]

            if s < NPT:
                xslot = s % 8
                dxp, c_dxp = dXPs[s % 4], c_dXPs[s % 4]

                def pf_load(e, s=s, xslot=xslot, dxp=dxp,
                            lr=(ln_ready[s - 8] if s >= 8 else 0)):
                    if lr:
                        e.wait_ge(vS, lr)
                    e.dma_start(xpbuf[:, xslot * D:(xslot + 1) * D],
                                xp.ap()[s * 128:(s + 1) * 128, :]).then_inc(dxp, 16)
                plan["sync"].append(pf_load)
                ld_cnt = c_dxp.add(16)

                jobs[s] = LNJob(xpbuf[:, xslot * D:(xslot + 1) * D],
                                lnpb[:, (s % 8) * D:((s % 8) + 1) * D],
                                s % 2, pre_s=[(dxp, ld_cnt)], pre_v=())
                jobs[s].emit_s1(plan)
                jobs[s].emit_v1(plan)

        def pf_bbf(e, gd=badd_ctr[n_grp - 1][NC_D - 1]):
            e.wait_ge(vS, gd)
            for jc in range(NC_D):
                ins = e.tensor_copy(glnp[:, jc * D:(jc + 1) * D],
                                    bsb[:, jc * D:(jc + 1) * D])
            ins.then_inc(vS, 1)
        plan["vector"].append(pf_bbf)
        v_bbf = c_v.add(1)

        wi_base = c_t.v
        winit_ev_pred = [c_v.v + 1 + ec for ec in range(NC_D)]

        def pf_winit(e, vb=v_bbf, dww=dw_wwt, wep=tuple(winit_ev_pred)):
            e.wait_ge(vS, vb)
            e.wait_ge(dW0, dww)
            for ec in range(NC_D):
                pb = pp[ec % 2]
                if ec >= 2:
                    e.wait_ge(vS, wep[ec - 2])
                last = None
                for jc in range(NC_D):
                    for hf in range(2):
                        last = e.matmul(
                            pb[:, hf * 512:(hf + 1) * 512],
                            glnp[:, jc * D + ec * 128: jc * D + (ec + 1) * 128],
                            wwt_sb[:, jc * D + hf * 512: jc * D + (hf + 1) * 512],
                            start=(jc == 0), stop=(jc == NC_D - 1))
                last.then_inc(tS, 1)
        plan["tensor"].append(pf_winit)
        c_t.add(NC_D)

        def pf_wev(e, wb=wi_base):
            for ec in range(NC_D):
                pb = pp[ec % 2]
                e.wait_ge(tS, wb + ec + 1)
                e.tensor_copy(wstate[:, ec * D:(ec + 1) * D],
                              pb[:, :]).then_inc(vS, 1)
        plan["vector"].append(pf_wev)
        winit_ev = [c_v.add(1) for _ in range(NC_D)]
        assert winit_ev == winit_ev_pred

        def pf_bnd(e, lr=ln_ready[NPT - 1], gm=grp_mm_done[n_grp - 1],
                   tgt=c_dK.v + 16):
            e.wait_ge(vS, lr)
            e.wait_ge(tS, gm)
            e.dma_start(bnd_bf[0:1, :],
                        lnpb[127:128, ((NPT - 1) % 8) * D:
                             ((NPT - 1) % 8 + 1) * D]).then_inc(dK, 16)
            e.wait_ge(dK, tgt)
        plan["gpsimd"].append(pf_bnd)
        c_dK.add(16)
        wi_done = c_t.v

    mainln = sb("mainln", [128, D], F32)
    lnb = sb("lnb", [128, 2 * D], BF16)
    ln2b = sb("ln2b", [128, D], BF16)
    ln1fm = sb("ln1fm", [128, NC_D * 129], BF16)
    ln2fm = sb("ln2fm", [128, NC_D * TILE], BF16)
    wk_tm = sb("wk_tm", [128, D], BF16)
    v_tm = sb("v_tm", [128, D], BF16)
    vgw = sb("vgw", [128, D], BF16)
    reads_bf = sb("reads_bf", [128, D], BF16)
    rkgp = sb("rkgp", [128, D], BF16)
    smt = sb("smt", [128, 128], BF16)
    silu_t = sb("silu_t", [128, 2 * TILE], BF16)
    xtile = overlay("xtile", [128, BPT * D], F32, 0)
    h_sb = overlay("h_sb", [128, NC_H * TILE], BF16, 16384)
    h2_sb = overlay("h2_sb", [128, NC_H * TILE], BF16, 49152)
    hbuf = overlay("hbuf", [128, 2 * DH], BF16, 81920)
    w12buf = overlay("w12buf", [128, 4 * D], BF16, 98304)
    w3buf = overlay("w3buf", [128, 4 * D], BF16, 106496)

    def mv_bnd(e, dk=c_dK.v):
        e.wait_ge(dK, dk)
        e.dma_start(lnb[127:128, D:2 * D], bnd_bf[0:1, :]).then_inc(dK, 16)
        for jc in range(NC_D):
            e.dma_start(ln1fm[:, jc * 129: jc * 129 + 1],
                        bnd_bf[0:1, jc * 128:(jc + 1) * 128]).then_inc(dK, 16)
        e.wait_ge(dK, dk + 16 * (1 + NC_D))
    plan["gpsimd"].append(mv_bnd)
    mv_cnt = c_dK.add(16 * (1 + NC_D))

    state_upd_v = [0] * NC_D
    prev_dmm = [0, 0]
    prev_resid_v = 0
    prev_ln2t_t = 0
    prev_lncast_s = 0
    prev_wkshift = c_dK.v
    prev_ffn_vec = 0
    pend_tr2 = None
    next_ln1 = None
    last_ln2ev = 0
    prev_store_cnt = 0

    prev_res_subs = [0] * BPT
    for t in range(N_TILE):

        def tl_x(e, t=t, prs=tuple(prev_res_subs),
                 lnl=(ln_ready[NPT - 1] if t == 0 else 0)):
            if lnl:
                e.wait_ge(vS, lnl)
            for b in range(BPT):
                g = t * BPT + b
                if prs[b]:
                    e.wait_ge(vS, prs[b])
                e.dma_start(xtile[:, b * D:(b + 1) * D],
                            xs.ap()[g * 128:(g + 1) * 128, :]
                            ).then_inc(dXPs[b], 16)
        plan["sync"].append(tl_x)
        x_sub = [c_dXPs[b].add(16) for b in range(BPT)]

        for b in range(BPT):
            g = t * BPT + b
            par = g % 2

            if b == 0:
                pw_v = [(dK, prev_wkshift)]
                if prev_lncast_s:
                    pw_v.append((sS, prev_lncast_s))
                ln1_v = emit_ln(plan, xtile[:, b * D:(b + 1) * D],
                                mainln[:, :],
                                pre_waits_scalar=[(dXPs[b], x_sub[b])],
                                pre_waits_vector=pw_v, par=0)

                def at_cast(e, par=par, lv=ln1_v, su=state_upd_v[NC_D - 1]):
                    e.wait_ge(vS, lv)
                    if su:
                        e.wait_ge(vS, su)
                    e.copy(lnb[:, par * D:(par + 1) * D],
                           mainln[:, :]).then_inc(sS, 1)
                plan["scalar"].append(at_cast)
                lncast_s = c_s.add(1)

                def at_eadd(e, b=b, cs=lncast_s):
                    e.wait_ge(sS, cs)
                    e.tensor_add(xtile[:, b * D:(b + 1) * D],
                                 xtile[:, b * D:(b + 1) * D],
                                 mainln[:, :]).then_inc(vS, 1)
                plan["vector"].append(at_eadd)
                c_v.add(1)
            else:
                ln1_v, lncast_s = next_ln1
            prev_lncast_s = lncast_s

            def at_tr(e, par=par, cs=lncast_s,
                      pf=(prev_ffn_vec if b == 0 else 0), le=last_ln2ev):
                e.wait_ge(sS, cs)
                if pf:
                    e.wait_ge(vS, pf)
                if le:
                    e.wait_ge(vS, le)
                last = None
                for jc in range(NC_D):
                    last = e.matmul(
                        p01[:, jc * 128:(jc + 1) * 128],
                        lnb[:, par * D + jc * 128: par * D + (jc + 1) * 128],
                        identb[:, :], start=True, stop=True)
                last.then_inc(tS, 1)
            plan["tensor"].append(at_tr)
            tr_done = c_t.add(1)

            def at_ev(e, td=tr_done, g=g):
                e.wait_ge(tS, td)
                if g > 0:
                    for jc in range(NC_D):
                        e.copy(ln1fm[:, jc * 129: jc * 129 + 1],
                               ln1fm[:, jc * 129 + 128: jc * 129 + 129])
                last = None
                for jc in range(NC_D):
                    last = e.copy(ln1fm[:, jc * 129 + 1: jc * 129 + 129],
                                  p01[:, jc * 128:(jc + 1) * 128])
                last.then_inc(sS, 1)
            plan["scalar"].append(at_ev)
            ev_done = c_s.add(1)

            def at_wk(e, par=par, cs=lncast_s, su=state_upd_v[NC_D - 1]):
                e.wait_ge(sS, cs)
                if su:
                    e.wait_ge(vS, su)
                e.dma_start(wk_tm[1:128, :],
                            lnb[0:127, par * D:(par + 1) * D]).then_inc(dK, 16)
                e.dma_start(wk_tm[0:1, :],
                            lnb[127:128, (1 - par) * D:(2 - par) * D]
                            ).then_inc(dK, 16)
            plan["gpsimd"].append(at_wk)
            wk_done = c_dK.add(32)
            prev_wkshift = wk_done

            def at_v(e, ed=ev_done, dww=dw_wwt,
                     wiv=(winit_ev[-1] if g == 0 else 0),
                     mvc=(mv_cnt if g == 0 else 0),
                     pv=(prev_ffn_vec if b == 0 else prev_resid_v)):
                e.wait_ge(sS, ed)
                e.wait_ge(dW0, dww)
                if wiv:
                    e.wait_ge(vS, wiv)
                if mvc:
                    e.wait_ge(dK, mvc)
                if pv:
                    e.wait_ge(vS, pv)
                last = None
                for jc in range(NC_D):
                    for hf in range(2):
                        last = e.matmul(
                            p2[:, hf * 512:(hf + 1) * 512],
                            ln1fm[:, jc * 129 + 1: jc * 129 + 129],
                            wwt_sb[:, jc * D + hf * 512: jc * D + (hf + 1) * 512],
                            start=(jc == 0), stop=(jc == NC_D - 1))
                last.then_inc(tS, 1)
            plan["tensor"].append(at_v)
            v_mm = c_t.add(1)

            def at_s(e, su=state_upd_v[NC_D - 2],
                     mvc=(mv_cnt if g == 0 else 0),
                     pf=(prev_ffn_vec if b == 0 else 0)):
                if su:
                    e.wait_ge(vS, su)
                if mvc:
                    e.wait_ge(dK, mvc)
                if pf:
                    e.wait_ge(vS, pf)
                last = None
                for jc in range(NC_D):
                    last = e.matmul(p4[:, 0:128],
                                    ln1fm[:, jc * 129: jc * 129 + 128],
                                    ln1fm[:, jc * 129 + 1: jc * 129 + 129],
                                    start=(jc == 0), stop=(jc == NC_D - 1))
                last.then_inc(tS, 1)
            plan["tensor"].append(at_s)
            s_mm = c_t.add(1)

            if pend_tr2 is not None:
                def at_tr2d(e, lv=pend_tr2[0], ed=ev_done):
                    e.wait_ge(vS, lv)
                    e.wait_ge(sS, ed)
                    last = None
                    for jc in range(NC_D):
                        last = e.matmul(
                            p01[:, jc * 128:(jc + 1) * 128],
                            ln2b[:, jc * 128:(jc + 1) * 128],
                            identb[:, :], start=True, stop=True)
                    last.then_inc(tS, 1)
                plan["tensor"].append(at_tr2d)
                tr2d_done = c_t.add(1)
                prev_ln2t_t = tr2d_done

            def at_vec1(e, vm=v_mm, sm=s_mm):
                e.wait_ge(tS, vm)
                e.tensor_copy(v_tm[:, :], p2[:, :])
                e.tensor_scalar_mul(vgw[:, :], v_tm[:, :], gw_col[:, 0:1])
                e.wait_ge(tS, sm)
                e.tensor_mul(smt[:, :], p4[:, 0:128], maskT[:, :])
                ins = None
                for jc in range(NC_D):
                    ins = e.tensor_mul(rkgp[:, jc * 128:(jc + 1) * 128],
                                       ln1fm[:, jc * 129 + 1: jc * 129 + 129],
                                       gp_bc[:, :])
                ins.then_inc(vS, 1)
            plan["vector"].append(at_vec1)
            vec1 = c_v.add(1)

            if pend_tr2 is not None:
                def at_ev2d(e, td=tr2d_done, bb=b - 1):
                    e.wait_ge(tS, td)
                    last = None
                    for jc in range(NC_D):
                        last = e.tensor_copy(
                            ln2fm[:, jc * TILE + bb * 128:
                                  jc * TILE + (bb + 1) * 128],
                            p01[:, jc * 128:(jc + 1) * 128])
                    last.then_inc(vS, 1)
                plan["vector"].append(at_ev2d)
                last_ln2ev = c_v.add(1)
                pend_tr2 = None

            if b + 1 < BPT:
                ln1n_v = emit_ln(plan, xtile[:, (b + 1) * D:(b + 2) * D],
                                 mainln[:, :],
                                 pre_waits_scalar=[(dXPs[b + 1], x_sub[b + 1])],
                                 pre_waits_vector=[(sS, lncast_s),
                                                   (dK, wk_done)], par=0)

                def at_castn(e, par=(g + 1) % 2, lv=ln1n_v,
                             su=state_upd_v[NC_D - 1], wd=wk_done):
                    e.wait_ge(vS, lv)
                    if su:
                        e.wait_ge(vS, su)
                    e.wait_ge(dK, wd)
                    e.copy(lnb[:, par * D:(par + 1) * D],
                           mainln[:, :]).then_inc(sS, 1)
                plan["scalar"].append(at_castn)
                next_ln1 = (ln1n_v, c_s.add(1))
                prev_lncast_s = next_ln1[1]

                def at_eaddn(e, b=b + 1, cs=next_ln1[1]):
                    e.wait_ge(sS, cs)
                    e.tensor_add(xtile[:, b * D:(b + 1) * D],
                                 xtile[:, b * D:(b + 1) * D],
                                 mainln[:, :]).then_inc(vS, 1)
                plan["vector"].append(at_eaddn)
                c_v.add(1)

            def at_reads(e, v1=vec1):
                e.wait_ge(vS, v1)
                last = None
                for ic in range(NC_D):
                    for jc in range(NC_D):
                        last = e.matmul(
                            p3[:, ic * 128:(ic + 1) * 128],
                            wstate[:, jc * D + ic * 128: jc * D + (ic + 1) * 128],
                            rkgp[:, jc * 128:(jc + 1) * 128],
                            start=(jc == 0), stop=False)
                    last = e.matmul(p3[:, ic * 128:(ic + 1) * 128],
                                    v_tm[:, ic * 128:(ic + 1) * 128],
                                    smt[:, :], start=False, stop=True)
                last.then_inc(tS, 1)
            plan["tensor"].append(at_reads)
            reads_mm = c_t.add(1)

            def at_rev(e, rm=reads_mm):
                e.wait_ge(tS, rm)
                e.tensor_copy(reads_bf[:, :], p3[:, :]).then_inc(vS, 1)
            plan["vector"].append(at_rev)
            rev_done = c_v.add(1)

            def at_ro(e, rv=rev_done):
                e.wait_ge(vS, rv)
                last = None
                for ic in range(NC_D):
                    for hf in range(2):
                        last = e.matmul(
                            p2[:, hf * 512:(hf + 1) * 512],
                            reads_bf[:, ic * 128:(ic + 1) * 128],
                            wrt_sb[:, ic * D + hf * 512: ic * D + (hf + 1) * 512],
                            start=(ic == 0), stop=(ic == NC_D - 1))
                last.then_inc(tS, 1)
            plan["tensor"].append(at_ro)
            ro_mm = c_t.add(1)

            def at_res(e, rm=ro_mm, b=b, par=par):
                e.wait_ge(tS, rm)
                e.tensor_add(xtile[:, b * D:(b + 1) * D],
                             xtile[:, b * D:(b + 1) * D],
                             p2[:, :]).then_inc(vS, 1)
            plan["vector"].append(at_res)
            prev_resid_v = c_v.add(1)

            upd_ev_pred = [c_v.v + 1 + jc for jc in range(NC_D)]

            def at_upd(e, wd=wk_done, v1=vec1, rv=rev_done,
                       uep=tuple(upd_ev_pred)):
                e.wait_ge(dK, wd)
                e.wait_ge(vS, v1)
                for jc in range(NC_D):
                    pb = [p4, p3][jc % 2]
                    if jc == 1:
                        e.wait_ge(vS, rv)
                    if jc >= 2:
                        e.wait_ge(vS, uep[jc - 2])
                    last = None
                    for hf in range(2):
                        last = e.matmul(pb[:, hf * 512:(hf + 1) * 512],
                                        wk_tm[:, jc * 128:(jc + 1) * 128],
                                        vgw[:, hf * 512:(hf + 1) * 512],
                                        start=True, stop=True)
                    last.then_inc(tS, 1)
            plan["tensor"].append(at_upd)
            upd_mm = [c_t.add(1) for _ in range(NC_D)]

            def at_updv(e, um=tuple(upd_mm)):
                for jc in range(NC_D):
                    pb = [p4, p3][jc % 2]
                    e.wait_ge(tS, um[jc])
                    e.scalar_tensor_tensor(wstate[:, jc * D:(jc + 1) * D],
                                           wstate[:, jc * D:(jc + 1) * D],
                                           gc_col[:, 0:1], pb[:, :],
                                           OP.mult, OP.add).then_inc(vS, 1)
            plan["vector"].append(at_updv)
            upd_ev = [c_v.add(1) for _ in range(NC_D)]
            assert upd_ev == upd_ev_pred
            state_upd_v = upd_ev

            pw_v2 = [(vS, prev_resid_v)]
            if prev_ln2t_t:
                pw_v2.append((tS, prev_ln2t_t))
            ln2_v = emit_ln(plan, xtile[:, b * D:(b + 1) * D],
                            ln2b[:, :],
                            pre_waits_scalar=[(vS, prev_resid_v)],
                            pre_waits_vector=pw_v2, par=1)

            if b + 1 < BPT:
                pend_tr2 = (ln2_v,)
            else:
                def at_tr2(e, par=par, lv=ln2_v, ed=ev_done, le=last_ln2ev):
                    e.wait_ge(vS, lv)
                    e.wait_ge(sS, ed)
                    if le:
                        e.wait_ge(vS, le)
                    last = None
                    for jc in range(NC_D):
                        last = e.matmul(
                            p01[:, jc * 128:(jc + 1) * 128],
                            ln2b[:, jc * 128:(jc + 1) * 128],
                            identb[:, :], start=True, stop=True)
                    last.then_inc(tS, 1)
                plan["tensor"].append(at_tr2)
                tr2_done = c_t.add(1)
                prev_ln2t_t = tr2_done

                def at_ev2(e, td=tr2_done, b=b):
                    e.wait_ge(tS, td)
                    last = None
                    for jc in range(NC_D):
                        last = e.tensor_copy(
                            ln2fm[:, jc * TILE + b * 128:
                                  jc * TILE + (b + 1) * 128],
                            p01[:, jc * 128:(jc + 1) * 128])
                    last.then_inc(vS, 1)
                plan["vector"].append(at_ev2)
                ln2ev = c_v.add(1)
                last_ln2ev = ln2ev

        ln2_ready = ln2ev

        wmm_done, h_done = [], []
        for oc in range(NC_H):
            par2 = oc % 2
            dw, cdw = dFs[par2], c_dFs[par2]

            if t == 0:
                dw, cdw = dWs[par2], c_dWs[par2]

                def fa_dma(e, oc=oc, par2=par2, dw=dw,
                           wm=(wmm_done[oc - 2] if oc >= 2 else wi_done),
                           sc=(c_dSCs[par2].v if oc >= 2 else 0)):
                    if wm:
                        e.wait_ge(tS, wm)
                    if sc:
                        e.wait_ge(dSCs[par2], sc)
                    e.dma_start(w12buf[:, par2 * 2 * D:(par2 + 1) * 2 * D],
                                w12.ap()[oc]).then_inc(dw, 16)
                plan["gpsimd"].append(fa_dma)
                dcnt = cdw.add(16)

                def fa_scr(e, oc=oc, par2=par2, dw=dw, dc=cdw.v):
                    e.wait_ge(dw, dc)
                    e.dma_start(sc12.ap()[oc],
                                w12buf[:, par2 * 2 * D:(par2 + 1) * 2 * D]
                                ).then_inc(dSCs[par2], 16)
                plan["sync"].append(fa_scr)
                c_dSCs[par2].add(16)
            else:
                def fa_dma(e, oc=oc, par2=par2, dw=dw,
                           wm=(wmm_done[oc - 2] if oc >= 2 else prev_dmm[oc])):
                    if wm:
                        e.wait_ge(tS, wm)
                    e.dma_start(w12buf[:, par2 * 2 * D:(par2 + 1) * 2 * D],
                                sc12.ap()[oc]).then_inc(dw, 16)
                plan["sync"].append(fa_dma)
                dcnt = cdw.add(16)

            def fa_mm(e, oc=oc, par2=par2, dw=dw, dcnt=dcnt, l2r=ln2_ready,
                      hd=(h_done[oc - 2] if oc >= 2 else 0)):
                e.wait_ge(dw, dcnt)
                if oc == 0:
                    e.wait_ge(vS, l2r)
                if hd:
                    e.wait_ge(vS, hd)
                pm = [p01, p3][par2]
                last = None
                base = par2 * 2 * D
                for jc in range(NC_D):
                    last = e.matmul(
                        pm[:, 0:512],
                        w12buf[:, base + jc * 128: base + (jc + 1) * 128],
                        ln2fm[:, jc * TILE: (jc + 1) * TILE],
                        start=(jc == 0), stop=(jc == NC_D - 1))
                for jc in range(NC_D):
                    last = e.matmul(
                        pm[:, 512:1024],
                        w12buf[:, base + D + jc * 128: base + D + (jc + 1) * 128],
                        ln2fm[:, jc * TILE: (jc + 1) * TILE],
                        start=(jc == 0), stop=(jc == NC_D - 1))
                last.then_inc(tS, 1)
            plan["tensor"].append(fa_mm)
            wmm_done.append(c_t.add(1))

            def fa_silu(e, oc=oc, par2=par2, wm=wmm_done[oc]):
                e.wait_ge(tS, wm)
                pm = [p01, p3][par2]
                e.activation(silu_t[:, par2 * TILE:(par2 + 1) * TILE],
                             pm[:, 0:512], AF.Silu,
                             scale=scl["w1s"][:, oc:oc + 1]).then_inc(sS, 1)
            plan["scalar"].append(fa_silu)
            silu_done = c_s.add(1)

            def fa_h(e, oc=oc, par2=par2, sd=silu_done):
                e.wait_ge(sS, sd)
                pm = [p01, p3][par2]
                e.scalar_tensor_tensor(h_sb[:, oc * TILE:(oc + 1) * TILE],
                                       pm[:, 512:1024],
                                       scl["w2s"][:, oc:oc + 1],
                                       silu_t[:, par2 * TILE:(par2 + 1) * TILE],
                                       OP.mult, OP.mult).then_inc(vS, 1)
            plan["vector"].append(fa_h)
            h_done.append(c_v.add(1))

        bmm_done, gel_done = [], []
        for oc in range(NC_H):
            par2 = oc % 2
            dw, cdw = dFs[par2], c_dFs[par2]

            if t == 0:
                dw, cdw = dWs[par2], c_dWs[par2]

                def fb_dma(e, oc=oc, par2=par2, dw=dw,
                           bm=(bmm_done[oc - 2] if oc >= 2
                               else wmm_done[NC_H - 2 + oc]),
                           sc=(c_dSCs[par2].v if oc >= 2 else 0)):
                    if bm:
                        e.wait_ge(tS, bm)
                    if sc:
                        e.wait_ge(dSCs[par2], sc)
                    e.dma_start(hbuf[:, par2 * DH:(par2 + 1) * DH],
                                h1b.ap()[oc]).then_inc(dw, 16)
                plan["gpsimd"].append(fb_dma)
                bcnt = cdw.add(16)

                def fb_scr(e, oc=oc, par2=par2, dw=dw, dc=cdw.v):
                    e.wait_ge(dw, dc)
                    e.dma_start(sch1.ap()[oc],
                                hbuf[:, par2 * DH:(par2 + 1) * DH]
                                ).then_inc(dSCs[par2], 16)
                plan["sync"].append(fb_scr)
                c_dSCs[par2].add(16)
            else:
                def fb_dma(e, oc=oc, par2=par2, dw=dw,
                           bm=(bmm_done[oc - 2] if oc >= 2
                               else wmm_done[NC_H - 2 + oc])):
                    if bm:
                        e.wait_ge(tS, bm)
                    e.dma_start(hbuf[:, par2 * DH:(par2 + 1) * DH],
                                sch1.ap()[oc]).then_inc(dw, 16)
                plan["sync"].append(fb_dma)
                bcnt = cdw.add(16)

            def fb_mm(e, oc=oc, par2=par2, dw=dw, dcnt=bcnt,
                      hlast=h_done[NC_H - 1],
                      gd=(gel_done[oc - 2] if oc >= 2 else 0)):
                e.wait_ge(dw, dcnt)
                if oc == 0:
                    e.wait_ge(vS, hlast)
                if gd:
                    e.wait_ge(sS, gd)
                pm2 = p2[:, par2 * 512:(par2 + 1) * 512]
                last = None
                for kc in range(NC_H):
                    last = e.matmul(
                        pm2,
                        hbuf[:, par2 * DH + kc * 128: par2 * DH + (kc + 1) * 128],
                        h_sb[:, kc * TILE:(kc + 1) * TILE],
                        start=(kc == 0), stop=(kc == NC_H - 1))
                last.then_inc(tS, 1)
            plan["tensor"].append(fb_mm)
            bmm_done.append(c_t.add(1))

            def fb_gelu(e, oc=oc, par2=par2, bm=bmm_done[oc]):
                e.wait_ge(tS, bm)
                pm2 = p2[:, par2 * 512:(par2 + 1) * 512]
                e.activation(h2_sb[:, oc * TILE:(oc + 1) * TILE], pm2, AF.Gelu,
                             scale=scl["h1s"][:, oc:oc + 1]).then_inc(sS, 1)
            plan["scalar"].append(fb_gelu)
            gel_done.append(c_s.add(1))

        cmm_done, gel2_done = [], []
        for oc in range(NC_H):
            par2 = oc % 2
            dw, cdw = dWs[par2], c_dWs[par2]

            def fc_dma(e, oc=oc, par2=par2, dw=dw,
                       cm=(cmm_done[oc - 2] if oc >= 2 else 0),
                       bm=(bmm_done[NC_H - 2 + oc] if oc < 2 else 0)):
                if cm:
                    e.wait_ge(tS, cm)
                if bm:
                    e.wait_ge(tS, bm)
                e.dma_start(hbuf[:, par2 * DH:(par2 + 1) * DH],
                            h2b.ap()[oc]).then_inc(dw, 16)
            plan["gpsimd"].append(fc_dma)
            ccnt = cdw.add(16)

            def fc_mm(e, oc=oc, par2=par2, dw=dw, dcnt=ccnt,
                      glast=gel_done[NC_H - 1],
                      g2=(gel2_done[oc - 2] if oc >= 2 else 0)):
                e.wait_ge(dw, dcnt)
                if oc == 0:
                    e.wait_ge(sS, glast)
                if g2:
                    e.wait_ge(sS, g2)
                pm2 = p2[:, par2 * 512:(par2 + 1) * 512]
                last = None
                for kc in range(NC_H):
                    last = e.matmul(
                        pm2,
                        hbuf[:, par2 * DH + kc * 128: par2 * DH + (kc + 1) * 128],
                        h2_sb[:, kc * TILE:(kc + 1) * TILE],
                        start=(kc == 0), stop=(kc == NC_H - 1))
                last.then_inc(tS, 1)
            plan["tensor"].append(fc_mm)
            cmm_done.append(c_t.add(1))

            def fc_gelu(e, oc=oc, par2=par2, cm=cmm_done[oc]):
                e.wait_ge(tS, cm)
                pm2 = p2[:, par2 * 512:(par2 + 1) * 512]
                e.activation(h_sb[:, oc * TILE:(oc + 1) * TILE], pm2, AF.Gelu,
                             scale=scl["h2s"][:, oc:oc + 1]).then_inc(sS, 1)
            plan["scalar"].append(fc_gelu)
            gel2_done.append(c_s.add(1))

        psO = [p01, p2, p3, p4]
        dmm_done = []
        for kp in range(NC_H // 2):
            par2 = kp % 2
            dw, cdw = dFs[par2], c_dFs[par2]

            if t == 0:
                dw, cdw = dWs[par2], c_dWs[par2]

                def fd_dma(e, kp=kp, par2=par2, dw=dw,
                           dm=(dmm_done[kp - 2] if kp >= 2 else
                               cmm_done[NC_H - 2 + kp] if kp < 2 else 0),
                           sc=c_dSCs[par2].v):
                    if dm:
                        e.wait_ge(tS, dm)
                    e.wait_ge(dSCs[par2], sc)
                    e.dma_start(w3buf[:, par2 * 2 * D:(par2 + 1) * 2 * D],
                                w3r.ap()[kp]).then_inc(dw, 16)
                plan["gpsimd"].append(fd_dma)
                dcnt3 = cdw.add(16)

                def fd_scr(e, kp=kp, par2=par2, dw=dw, dc=cdw.v):
                    e.wait_ge(dw, dc)
                    e.dma_start(sc3.ap()[kp],
                                w3buf[:, par2 * 2 * D:(par2 + 1) * 2 * D]
                                ).then_inc(dSCs[par2], 16)
                plan["sync"].append(fd_scr)
                c_dSCs[par2].add(16)
            else:
                def fd_dma(e, kp=kp, par2=par2, dw=dw,
                           dm=(dmm_done[kp - 2] if kp >= 2 else
                               cmm_done[NC_H - 2 + kp] if kp < 2 else 0)):
                    if dm:
                        e.wait_ge(tS, dm)
                    e.dma_start(w3buf[:, par2 * 2 * D:(par2 + 1) * 2 * D],
                                sc3.ap()[kp]).then_inc(dw, 16)
                plan["sync"].append(fd_dma)
                dcnt3 = cdw.add(16)

            def fd_mm(e, kp=kp, par2=par2, dw=dw, dcnt=dcnt3,
                      g2last=gel2_done[NC_H - 1]):
                e.wait_ge(dw, dcnt)
                if kp == 0:
                    e.wait_ge(sS, g2last)
                last = None
                for j in range(2):
                    kc = 2 * kp + j
                    for sub in range(BPT):
                        for hf in range(2):
                            last = e.matmul(
                                psO[sub][:, hf * 512:(hf + 1) * 512],
                                h_sb[:, kc * TILE + sub * 128:
                                     kc * TILE + (sub + 1) * 128],
                                w3buf[:, par2 * 2 * D + j * D + hf * 512:
                                      par2 * 2 * D + j * D + (hf + 1) * 512],
                                start=(kc == 0), stop=(kc == NC_H - 1))
                last.then_inc(tS, 1)
            plan["tensor"].append(fd_mm)
            dmm_done.append(c_t.add(1))

        def fd_res(e, dm=dmm_done[NC_H // 2 - 1]):
            e.wait_ge(tS, dm)
            for sub in range(BPT):
                e.tensor_mul(psO[sub][:, :], psO[sub][:, :], s3bc[:, :])
                e.tensor_add(xtile[:, sub * D:(sub + 1) * D],
                             xtile[:, sub * D:(sub + 1) * D],
                             psO[sub][:, :]).then_inc(vS, 1)
        plan["vector"].append(fd_res)
        res_done = [c_v.add(1) for _ in range(BPT)]
        prev_ffn_vec = res_done[-1]
        prev_res_subs = res_done

        def fd_store(e, t=t, rd=tuple(res_done)):
            for sub in range(BPT):
                g = t * BPT + sub
                e.wait_ge(vS, rd[sub])
                e.dma_start(o_seg.ap()[g * 128:(g + 1) * 128, :],
                            xtile[:, sub * D:(sub + 1) * D]).then_inc(dO, 16)
        plan["sync"].append(fd_store)
        prev_store_cnt = c_dO.add(16 * BPT)
        prev_dmm = [dmm_done[NC_H // 2 - 2], dmm_done[NC_H // 2 - 1]]

    def fin(e, total=c_dO.v):
        e.wait_ge(dO, total)
    plan["sync"].append(fin)
    run_block(plan)

    es.close()
    return nc


_NC_CACHE = None


def prep_in_maps(L0_attn_out, L0_attn_W_write, L0_attn_W_read, L0_attn_decay,
                 L0_attn_log_alpha, L0_ffn_w1_w, L0_ffn_w1_s, L0_ffn_w2_w,
                 L0_ffn_w2_s, L0_ffn_h1_w, L0_ffn_h1_s, L0_ffn_h2_w,
                 L0_ffn_h2_s, L0_ffn_w3_w, L0_ffn_w3_s):
    x = np.ascontiguousarray(np.asarray(L0_attn_out, dtype=np.float32))
    B, T, _ = x.shape

    cont = np.ascontiguousarray
    wwt = cont(np.asarray(L0_attn_W_write, np.float32).T)
    wrt = cont(np.asarray(L0_attn_W_read, np.float32).T)
    w1b = (np.asarray(L0_ffn_w1_w, np.float32)
           .reshape(NC_H, 128, NC_D, 128).transpose(0, 3, 2, 1)
           .reshape(NC_H, 128, NC_D * 128))
    w2b = (np.asarray(L0_ffn_w2_w, np.float32)
           .reshape(NC_H, 128, NC_D, 128).transpose(0, 3, 2, 1)
           .reshape(NC_H, 128, NC_D * 128))
    w12 = cont(np.concatenate([w1b, w2b], axis=2))
    h1b = cont(np.asarray(L0_ffn_h1_w, np.float32)
               .reshape(NC_H, 128, NC_H, 128).transpose(0, 3, 2, 1)
               .reshape(NC_H, 128, NC_H * 128))
    h2b = cont(np.asarray(L0_ffn_h2_w, np.float32)
               .reshape(NC_H, 128, NC_H, 128).transpose(0, 3, 2, 1)
               .reshape(NC_H, 128, NC_H * 128))
    w3r = cont(np.asarray(L0_ffn_w3_w, np.float32).T
               .reshape(NC_H // 2, 2, 128, D).transpose(0, 2, 1, 3)
               .reshape(NC_H // 2, 128, 2 * D))
    w1s = cont(np.asarray(L0_ffn_w1_s, np.float32).reshape(NC_H, 128).T)
    w2s = cont(np.asarray(L0_ffn_w2_s, np.float32).reshape(NC_H, 128).T)
    h1s = cont(np.asarray(L0_ffn_h1_s, np.float32).reshape(NC_H, 128).T)
    h2s = cont(np.asarray(L0_ffn_h2_s, np.float32).reshape(NC_H, 128).T)
    s3bc = cont(np.broadcast_to(np.asarray(L0_ffn_w3_s, np.float32)[None, :],
                                (128, D)))
    dec = np.full((128, 1), np.float32(np.asarray(L0_attn_decay).reshape(())),
                  dtype=np.float32)
    la = np.full((128, 1), np.float32(np.asarray(L0_attn_log_alpha).reshape(())),
                 dtype=np.float32)

    in_maps = []
    for core in range(8):
        b, s = core // 4, core % 4
        seg0 = s * S_LEN
        xs_i = cont(x[b, seg0:seg0 + S_LEN])
        xp_i = np.zeros((P_LEN, D), np.float32)
        n_real = min(seg0, P_LEN)
        if n_real > 0:
            xp_i[P_LEN - n_real:] = x[b, seg0 - n_real:seg0]
        in_maps.append({
            "xs": xs_i, "xp": xp_i, "wwt": wwt, "wrt": wrt,
            "w12": w12, "h1b": h1b, "h2b": h2b, "w3r": w3r,
            "w1s": w1s, "w2s": w2s, "h1s": h1s, "h2s": h2s, "s3bc": s3bc,
            "dec": dec, "la": la,
        })
    return in_maps


def kernel(**inputs):
    global _NC_CACHE
    if _NC_CACHE is None:
        _NC_CACHE = build_nc()
    nc = _NC_CACHE
    in_maps = prep_in_maps(**inputs)
    res = run_bass_kernel_spmd(nc, in_maps, core_ids=list(range(8)))
    B = 2
    out = np.empty((B, B * 4 * S_LEN // B, D), np.float32)
    for core in range(8):
        b, s = core // 4, core % 4
        out[b, s * S_LEN:(s + 1) * S_LEN] = res.results[core]["o_seg"]
    return out


# version 24
# speedup vs baseline: 1.0139x; 1.0139x over previous
import numpy as np
from contextlib import ExitStack

import concourse.bass as bass
import concourse.mybir as mybir
from concourse.bass_utils import run_bass_kernel_spmd

F32 = mybir.dt.float32
BF16 = mybir.dt.bfloat16
AF = mybir.ActivationFunctionType
OP = mybir.AluOpType

D = 1024
DH = 4096
S_LEN = 2048
P_LEN = 6144
BC = 128
TILE = 512
N_TILE = S_LEN // TILE
BPT = TILE // BC
NPT = P_LEN // BC
PGRP = 8
NC_D = D // 128
NC_H = DH // 128


class Ctr:
    def __init__(self):
        self.v = 0

    def add(self, n):
        self.v += n
        return self.v


def build_nc():
    nc = bass.Bass("TRN2", target_bir_lowering=False, detect_race_conditions=False)

    xs = nc.dram_tensor("xs", [S_LEN, D], F32, kind="ExternalInput")
    xp = nc.dram_tensor("xp", [P_LEN, D], F32, kind="ExternalInput")
    wwt = nc.dram_tensor("wwt", [D, D], F32, kind="ExternalInput")
    wrt = nc.dram_tensor("wrt", [D, D], F32, kind="ExternalInput")
    w12 = nc.dram_tensor("w12", [NC_H, 128, 2 * NC_D * 128], F32, kind="ExternalInput")
    h1b = nc.dram_tensor("h1b", [NC_H, 128, NC_H * 128], F32, kind="ExternalInput")
    h2b = nc.dram_tensor("h2b", [NC_H, 128, NC_H * 128], F32, kind="ExternalInput")
    w3r = nc.dram_tensor("w3r", [NC_H // 2, 128, 2 * D], F32, kind="ExternalInput")
    w1s_d = nc.dram_tensor("w1s", [128, NC_H], F32, kind="ExternalInput")
    w2s_d = nc.dram_tensor("w2s", [128, NC_H], F32, kind="ExternalInput")
    h1s_d = nc.dram_tensor("h1s", [128, NC_H], F32, kind="ExternalInput")
    h2s_d = nc.dram_tensor("h2s", [128, NC_H], F32, kind="ExternalInput")
    s3bc_d = nc.dram_tensor("s3bc", [128, D], F32, kind="ExternalInput")
    dec = nc.dram_tensor("dec", [128, 1], F32, kind="ExternalInput")
    la = nc.dram_tensor("la", [128, 1], F32, kind="ExternalInput")
    o_seg = nc.dram_tensor("o_seg", [S_LEN, D], F32, kind="ExternalOutput")
    sc12 = nc.dram_tensor("sc12", [NC_H, 128, 2 * NC_D * 128], BF16)
    sch1 = nc.dram_tensor("sch1", [NC_H, 128, NC_H * 128], BF16)
    sch2 = nc.dram_tensor("sch2", [NC_H, 128, NC_H * 128], BF16)
    sc3 = nc.dram_tensor("sc3", [NC_H // 2, 128, 2 * D], BF16)

    es = ExitStack()
    sb = lambda name, shape, dt: es.enter_context(nc.sbuf_tensor(name, shape, dt))
    psum = lambda name, shape: es.enter_context(nc.psum_tensor(name, shape, F32))
    sem = lambda name: es.enter_context(nc.semaphore(name))

    dX = sem("dX")
    dXP0 = sem("dXP0")
    dXP1 = sem("dXP1")
    dXP2 = sem("dXP2")
    dXP3 = sem("dXP3")
    dO = sem("dO")
    dC = sem("dC")
    dWR = sem("dWR")
    dW0 = sem("dW0")
    dW1 = sem("dW1")
    dF0 = sem("dF0")
    dF1 = sem("dF1")
    dSC0 = sem("dSC0")
    dSC1 = sem("dSC1")
    dK = sem("dK")
    dK0 = sem("dK0")
    dK1 = sem("dK1")
    dK2 = sem("dK2")
    dK3 = sem("dK3")
    tS = sem("tS")
    vS = sem("vS")
    sS = sem("sS")
    gS = sem("gS")
    c_dX, c_dO, c_dC = Ctr(), Ctr(), Ctr()
    dXPs = [dXP0, dXP1, dXP2, dXP3]
    c_dXPs = [Ctr() for _ in range(4)]
    c_dW0, c_dW1, c_dK, c_dWR = Ctr(), Ctr(), Ctr(), Ctr()
    dKs = [dK0, dK1, dK2, dK3]
    c_dKs = [Ctr() for _ in range(4)]
    dFs = [dF0, dF1]
    c_dFs = [Ctr(), Ctr()]
    dSCs = [dSC0, dSC1]
    c_dSCs = [Ctr(), Ctr()]
    c_t, c_v, c_s, c_g = Ctr(), Ctr(), Ctr(), Ctr()
    dWs = [dW0, dW1]
    c_dWs = [c_dW0, c_dW1]

    wwt_sb = sb("wwt_sb", [128, NC_D * D], BF16)
    wrt_sb = sb("wrt_sb", [128, NC_D * D], BF16)
    wstate = sb("wstate", [128, NC_D * D], BF16)
    s3bc = sb("s3bc_sb", [128, D], F32)
    scl = {nm: sb(nm + "_sb", [128, NC_H], F32)
           for nm in ("w1s", "w2s", "h1s", "h2s")}
    dec_sb = sb("dec_sb", [128, 1], F32)
    la_sb = sb("la_sb", [128, 1], F32)
    gam = sb("gam", [128, 1], F32)
    lg = sb("lg", [128, 1], F32)
    alpha = sb("alpha", [128, 1], F32)
    gw_col = sb("gw_col", [128, 1], F32)
    gc_col = sb("gc_col", [128, 1], F32)
    eps_c = sb("eps_c", [128, 1], F32)
    nh_c = sb("nh_c", [128, 1], F32)
    i127 = sb("i127", [128, 1], F32)
    c128 = sb("c128", [128, 1], F32)
    kcol = sb("kcol", [128, 1], F32)
    icf = sb("icf", [128, 128], F32)
    identb = sb("identb", [128, 128], BF16)
    maskT = sb("maskT", [128, 128], F32)
    gp_bc = sb("gp_bc", [128, 128], BF16)
    j48 = sb("j48", [128, NPT], F32)
    e48 = sb("e48", [128, NPT], F32)
    gpref = sb("gpref", [128, NPT], F32)
    zrow = sb("zrow", [1, D], BF16)
    bnd_bf = sb("bnd_bf", [1, D], BF16)
    srow = sb("srow", [128, 2], F32)
    qrow = sb("qrow", [128, 2], F32)
    mcol = sb("mcol", [128, 2], F32)
    msq = sb("msq", [128, 2], F32)
    varc = sb("varc", [128, 2], F32)
    rstd = sb("rstd", [128, 2], F32)
    junk = sb("junk", [128, D], BF16)

    ARENA_B = 114688
    parena = sb("parena", [128, ARENA_B], mybir.dt.uint8)

    def overlay(name, shape, dtype, off):
        h = nc.virtual_sbuf_tensor(name, shape, dtype)
        nc._tensor(h.name, shape, dtype, type="SB")
        mloc = nc.lookup_mloc(h)
        mloc.allocated = not nc.target_bir_lowering
        mloc.addr = nc.lookup_mloc(parena).addr + off
        return h

    p01 = psum("p01", [128, 1024])
    p2 = psum("p2", [128, 1024])
    p3 = psum("p3", [128, 1024])
    p4 = psum("p4", [128, 1024])

    def run_block(plan):
        with nc.Block() as blk:
            for eng in ("sync", "gpsimd", "scalar", "vector", "tensor"):
                fns = plan.get(eng)
                if fns:
                    getattr(blk, eng)(lambda e, fns=fns: [f(e) for f in fns])

    def mk_plan():
        return {k: [] for k in ("sync", "gpsimd", "scalar", "vector", "tensor")}

    class LNJob:
        def __init__(self, xap, out_ap, par, pre_s=(), pre_v=()):
            self.xap, self.oap, self.par = xap, out_ap, par
            self.pre_s, self.pre_v = tuple(pre_s), tuple(pre_v)

        def emit_s1(self, plan):
            xap, w, p = self.xap, self.pre_s, self.par

            def s1(e):
                for sm, val in w:
                    e.wait_ge(sm, val)
                e.activation(junk[:, :], xap, AF.Copy,
                             accum_out=srow[:, p:p + 1])
                e.activation(junk[:, :], xap, AF.Square,
                             accum_out=qrow[:, p:p + 1]).then_inc(sS, 1)
            plan["scalar"].append(s1)
            self.s_done = c_s.add(1)

        def emit_v1(self, plan):
            w, sd, p = self.pre_v, self.s_done, self.par
            vself = c_v.v + 1

            def v1(e):
                for sm, val in w:
                    e.wait_ge(sm, val)
                e.wait_ge(sS, sd)
                e.tensor_scalar(msq[:, p:p + 1], srow[:, p:p + 1],
                                srow[:, p:p + 1], 1.0 / (D * D),
                                OP.mult, OP.mult)
                e.tensor_scalar(mcol[:, p:p + 1], srow[:, p:p + 1], 1.0 / D,
                                None, OP.mult).then_inc(vS, 1)
                e.wait_ge(vS, vself)
                e.scalar_tensor_tensor(varc[:, p:p + 1], qrow[:, p:p + 1],
                                       1.0 / D, msq[:, p:p + 1],
                                       OP.mult, OP.subtract).then_inc(vS, 1)
            plan["vector"].append(v1)
            c_v.add(1)
            self.v_done = c_v.add(1)

        def emit_s2(self, plan):
            vd, p = self.v_done, self.par
            sself = c_s.v + 1

            def s2(e):
                e.wait_ge(vS, vd)
                e.activation(varc[:, p:p + 1], varc[:, p:p + 1], AF.Ln,
                             bias=eps_c[:, 0:1]).then_inc(sS, 1)
                e.wait_ge(sS, sself)
                e.activation(rstd[:, p:p + 1], varc[:, p:p + 1], AF.Exp,
                             scale=nh_c[:, 0:1]).then_inc(sS, 1)
            plan["scalar"].append(s2)
            c_s.add(1)
            self.s2_done = c_s.add(1)

        def emit_v2(self, plan):
            xap, oap, sd, p = self.xap, self.oap, self.s2_done, self.par
            ev = getattr(self, 'extra_v2', None)

            def v2(e, ev=ev):
                e.wait_ge(sS, sd)
                if ev:
                    e.wait_ge(ev[0], ev[1])
                e.tensor_scalar(oap, xap, mcol[:, p:p + 1], rstd[:, p:p + 1],
                                OP.subtract, OP.mult).then_inc(vS, 1)
            plan["vector"].append(v2)
            self.ln_ready = c_v.add(1)
            return self.ln_ready

    def emit_ln(plan, xap, out_ap, pre_waits_scalar=(), pre_waits_vector=(),
                par=0):
        job = LNJob(xap, out_ap, par, pre_waits_scalar, pre_waits_vector)
        job.emit_s1(plan)
        job.emit_v1(plan)
        job.emit_s2(plan)
        return job.emit_v2(plan)

    setup_tmp = ExitStack()
    dm1 = setup_tmp.enter_context(nc.sbuf_tensor("dm1", [128, 128], F32))
    dcl = setup_tmp.enter_context(nc.sbuf_tensor("dcl", [128, 128], F32))
    caus = setup_tmp.enter_context(nc.sbuf_tensor("caus", [128, 128], F32))
    expd = setup_tmp.enter_context(nc.sbuf_tensor("expd", [128, 128], F32))
    plan = mk_plan()

    def su_sync(e):
        e.dma_start(dec_sb[:, :], dec.ap()).then_inc(dC, 16)
        e.dma_start(la_sb[:, :], la.ap()).then_inc(dC, 16)
        for sbuf, dr in ((scl["w1s"], w1s_d), (scl["w2s"], w2s_d),
                         (scl["h1s"], h1s_d), (scl["h2s"], h2s_d)):
            e.dma_start(sbuf[:, :], dr.ap()).then_inc(dC, 16)
        e.dma_start(s3bc[:, :], s3bc_d.ap()).then_inc(dC, 16)
    plan["sync"].append(su_sync)
    dc_base = c_dC.add(16 * 7)

    def su_gp(e):
        e.iota(icf[:, :], [[1, 128]], channel_multiplier=0,
               allow_small_or_imprecise_dtypes=True)
        e.iota(kcol[:, 0:1], [[0, 1]], channel_multiplier=1,
               allow_small_or_imprecise_dtypes=True)
        e.iota(j48[:, :], [[128, NPT]], channel_multiplier=1,
               allow_small_or_imprecise_dtypes=True)
        e.memset(c128[:, 0:1], float(BC))
        e.memset(eps_c[:, 0:1], 1e-5)
        e.memset(nh_c[:, 0:1], -0.5)
        e.memset(zrow[0:1, :], 0.0).then_inc(gS, 1)
        for jc in range(NC_D):
            e.dma_start(wwt_sb[:, jc * D:(jc + 1) * D],
                        wwt.ap()[jc * 128:(jc + 1) * 128, :]).then_inc(dW0, 16)
    plan["gpsimd"].append(su_gp)
    g_setup = c_g.add(1)
    dw_wwt = c_dW0.add(16 * NC_D)

    def su_scalar(e):
        e.wait_ge(dC, dc_base)
        e.wait_ge(gS, g_setup)
        e.activation(gam[:, 0:1], dec_sb[:, 0:1], AF.Sigmoid)
        e.activation(alpha[:, 0:1], la_sb[:, 0:1], AF.Exp).then_inc(sS, 1)
    plan["scalar"].append(su_scalar)
    s_ga = c_s.add(1)

    def su_vec1(e, sg=s_ga):
        e.wait_ge(sS, sg)
        e.tensor_scalar_max(gam[:, 0:1], gam[:, 0:1], 1e-30)
        e.tensor_scalar(i127[:, 0:1], kcol[:, 0:1], -1.0, 127.0, OP.mult, OP.add)
        e.tensor_scalar(dm1[:, :], icf[:, :], kcol[:, 0:1], 1.0,
                        OP.subtract, OP.subtract)
        e.tensor_scalar_max(dcl[:, :], dm1[:, :], 0.0)
        e.tensor_scalar(caus[:, :], dm1[:, :], 0.0, None, OP.is_ge)
        e.tensor_scalar(e48[:, :], j48[:, :], -1.0, float(P_LEN - 1),
                        OP.mult, OP.add)
        e.tensor_scalar(identb[:, :], icf[:, :], kcol[:, 0:1], None,
                        OP.is_equal).then_inc(vS, 1)
    plan["vector"].append(su_vec1)
    v_pre = c_v.add(1)

    def su_scalar3(e, vp=v_pre, ss=c_s.v + 1):
        e.wait_ge(vS, vp)
        e.activation(lg[:, 0:1], gam[:, 0:1], AF.Ln).then_inc(sS, 1)
        e.wait_ge(sS, ss)
        e.activation(gw_col[:, 0:1], i127[:, 0:1], AF.Exp, scale=lg[:, 0:1])
        e.activation(gc_col[:, 0:1], c128[:, 0:1], AF.Exp, scale=lg[:, 0:1])
        e.activation(expd[:, :], dcl[:, :], AF.Exp, scale=lg[:, 0:1])
        e.activation(gpref[:, :], e48[:, :], AF.Exp, scale=lg[:, 0:1])
        e.activation(gp_bc[:, :], icf[:, :], AF.Exp,
                     scale=lg[:, 0:1]).then_inc(sS, 1)
    plan["scalar"].append(su_scalar3)
    c_s.add(1)
    s_exp = c_s.add(1)

    def su_vec2(e, se=s_exp):
        e.wait_ge(sS, se)
        e.tensor_mul(maskT[:, :], expd[:, :], caus[:, :]).then_inc(vS, 1)
    plan["vector"].append(su_vec2)
    c_v.add(1)

    def su_wr_load(e):
        for ic in range(NC_D):
            e.dma_start(wrt_sb[:, ic * D:(ic + 1) * D],
                        wrt.ap()[ic * 128:(ic + 1) * 128, :]).then_inc(dW0, 16)
    plan["gpsimd"].append(su_wr_load)
    dw_wrt = c_dW0.add(16 * NC_D)

    def su_wr_scale(e, dcn=dw_wrt, sg=s_ga):
        e.wait_ge(dW0, dcn)
        e.wait_ge(sS, sg)
        e.tensor_scalar_mul(wrt_sb[:, :], wrt_sb[:, :],
                            alpha[:, 0:1]).then_inc(vS, 1)
    plan["vector"].append(su_wr_scale)
    c_v.add(1)

    run_block(plan)
    setup_tmp.close()

    if True:
        xpbuf = overlay("xpbuf", [128, 8 * D], F32, 0)
        lnpb = overlay("lnpb", [128, 8 * D], BF16, 32768)
        glnp = overlay("glnp", [128, PGRP * D], BF16, 49152)
        wkp = overlay("wkp", [128, PGRP * D], BF16, 65536)
        bsb = overlay("bsb", [128, NC_D * D], F32, 81920)
        plan = mk_plan()
        n_grp = NPT // PGRP
        ln_ready = [0] * NPT
        gl_ready = [0] * NPT
        wk_ready = [0] * NPT
        grp_mm_done = [0] * n_grp
        badd_ctr = {}
        pp = [p2, p3]
        jobs = [None] * NPT

        for s in range(NPT + 1):
            if s >= 1:
                q = s - 1
                par8 = q % 8
                par4 = q % 4
                slot = q % PGRP
                grp = q // PGRP
                jobs[q].emit_s2(plan)
                if q >= 7:
                    jobs[q].extra_v2 = (dKs[(q - 7) % 4], wk_ready[q - 7])
                ln_ready[q] = jobs[q].emit_v2(plan)

                def pf_gl(e, q=q, par8=par8, slot=slot,
                          gm=(grp_mm_done[grp - 1] if grp >= 1 else 0)):
                    if gm:
                        e.wait_ge(tS, gm)
                    e.tensor_scalar_mul(glnp[:, slot * D:(slot + 1) * D],
                                        lnpb[:, par8 * D:(par8 + 1) * D],
                                        gpref[:, q:q + 1]).then_inc(vS, 1)
                plan["vector"].append(pf_gl)
                gl_ready[q] = c_v.add(1)

                dk, c_dk = dKs[par4], c_dKs[par4]

                def pf_wk(e, q=q, par8=par8, slot=slot, lr=ln_ready[q], dk=dk,
                          gm=(grp_mm_done[grp - 1] if grp >= 1 else 0)):
                    e.wait_ge(vS, lr)
                    if gm:
                        e.wait_ge(tS, gm)
                    e.dma_start(wkp[1:128, slot * D:(slot + 1) * D],
                                lnpb[0:127, par8 * D:(par8 + 1) * D]
                                ).then_inc(dk, 16)
                    if q == 0:
                        e.dma_start(wkp[0:1, slot * D:(slot + 1) * D],
                                    zrow[0:1, :]).then_inc(dk, 16)
                    else:
                        e.dma_start(wkp[0:1, slot * D:(slot + 1) * D],
                                    lnpb[127:128, ((q - 1) % 8) * D:
                                         ((q - 1) % 8 + 1) * D]).then_inc(dk, 16)
                plan["gpsimd"].append(pf_wk)
                wk_ready[q] = c_dk.add(32)

                if slot == PGRP - 1:
                    wk_counts = [c_dKs[i].v for i in range(4)]

                    def pf_mm(e, grp=grp, glr=gl_ready[q], wkc=tuple(wk_counts)):
                        e.wait_ge(vS, glr)
                        for i in range(4):
                            e.wait_ge(dKs[i], wkc[i])
                        for jc in range(NC_D):
                            pb = pp[jc % 2]
                            if jc >= 2:
                                e.wait_ge(vS, badd_ctr[grp][jc - 2])
                            elif grp >= 1:
                                e.wait_ge(vS, badd_ctr[grp - 1][NC_D + jc - 2])
                            last = None
                            for t8 in range(PGRP):
                                for hf in range(2):
                                    last = e.matmul(
                                        pb[:, hf * 512:(hf + 1) * 512],
                                        glnp[:, t8 * D + jc * 128:
                                             t8 * D + (jc + 1) * 128],
                                        wkp[:, t8 * D + hf * 512:
                                            t8 * D + (hf + 1) * 512],
                                        start=(t8 == 0), stop=(t8 == PGRP - 1))
                            last.then_inc(tS, 1)
                    plan["tensor"].append(pf_mm)
                    mm_base = c_t.v
                    c_t.add(NC_D)
                    grp_mm_done[grp] = c_t.v

                    def pf_badd(e, grp=grp, mb=mm_base):
                        for jc in range(NC_D):
                            pb = pp[jc % 2]
                            e.wait_ge(tS, mb + jc + 1)
                            if grp == 0:
                                e.tensor_copy(bsb[:, jc * D:(jc + 1) * D],
                                              pb[:, :]).then_inc(vS, 1)
                            else:
                                e.tensor_add(bsb[:, jc * D:(jc + 1) * D],
                                             bsb[:, jc * D:(jc + 1) * D],
                                             pb[:, :]).then_inc(vS, 1)
                    plan["vector"].append(pf_badd)
                    badd_ctr[grp] = [c_v.add(1) for _ in range(NC_D)]

            if s < NPT:
                xslot = s % 8
                dxp, c_dxp = dXPs[s % 4], c_dXPs[s % 4]

                def pf_load(e, s=s, xslot=xslot, dxp=dxp,
                            lr=(ln_ready[s - 8] if s >= 8 else 0)):
                    if lr:
                        e.wait_ge(vS, lr)
                    e.dma_start(xpbuf[:, xslot * D:(xslot + 1) * D],
                                xp.ap()[s * 128:(s + 1) * 128, :]).then_inc(dxp, 16)
                plan["sync"].append(pf_load)
                ld_cnt = c_dxp.add(16)

                jobs[s] = LNJob(xpbuf[:, xslot * D:(xslot + 1) * D],
                                lnpb[:, (s % 8) * D:((s % 8) + 1) * D],
                                s % 2, pre_s=[(dxp, ld_cnt)], pre_v=())
                jobs[s].emit_s1(plan)
                jobs[s].emit_v1(plan)

        def pf_bbf(e, gd=badd_ctr[n_grp - 1][NC_D - 1]):
            e.wait_ge(vS, gd)
            for jc in range(NC_D):
                ins = e.tensor_copy(glnp[:, jc * D:(jc + 1) * D],
                                    bsb[:, jc * D:(jc + 1) * D])
            ins.then_inc(vS, 1)
        plan["vector"].append(pf_bbf)
        v_bbf = c_v.add(1)

        wi_base = c_t.v
        winit_ev_pred = [c_v.v + 1 + ec for ec in range(NC_D)]

        def pf_winit(e, vb=v_bbf, dww=dw_wwt, wep=tuple(winit_ev_pred)):
            e.wait_ge(vS, vb)
            e.wait_ge(dW0, dww)
            for ec in range(NC_D):
                pb = pp[ec % 2]
                if ec >= 2:
                    e.wait_ge(vS, wep[ec - 2])
                last = None
                for jc in range(NC_D):
                    for hf in range(2):
                        last = e.matmul(
                            pb[:, hf * 512:(hf + 1) * 512],
                            glnp[:, jc * D + ec * 128: jc * D + (ec + 1) * 128],
                            wwt_sb[:, jc * D + hf * 512: jc * D + (hf + 1) * 512],
                            start=(jc == 0), stop=(jc == NC_D - 1))
                last.then_inc(tS, 1)
        plan["tensor"].append(pf_winit)
        c_t.add(NC_D)

        def pf_wev(e, wb=wi_base):
            for ec in range(NC_D):
                pb = pp[ec % 2]
                e.wait_ge(tS, wb + ec + 1)
                e.tensor_copy(wstate[:, ec * D:(ec + 1) * D],
                              pb[:, :]).then_inc(vS, 1)
        plan["vector"].append(pf_wev)
        winit_ev = [c_v.add(1) for _ in range(NC_D)]
        assert winit_ev == winit_ev_pred

        def pf_bnd(e, lr=ln_ready[NPT - 1], gm=grp_mm_done[n_grp - 1],
                   tgt=c_dK.v + 16):
            e.wait_ge(vS, lr)
            e.wait_ge(tS, gm)
            e.dma_start(bnd_bf[0:1, :],
                        lnpb[127:128, ((NPT - 1) % 8) * D:
                             ((NPT - 1) % 8 + 1) * D]).then_inc(dK, 16)
            e.wait_ge(dK, tgt)
        plan["gpsimd"].append(pf_bnd)
        c_dK.add(16)
        wi_done = c_t.v

    mainln = sb("mainln", [128, D], F32)
    lnb = sb("lnb", [128, 2 * D], BF16)
    ln2b = sb("ln2b", [128, D], BF16)
    ln1fm = sb("ln1fm", [128, NC_D * 129], BF16)
    ln2fm = sb("ln2fm", [128, NC_D * TILE], BF16)
    wk_tm = sb("wk_tm", [128, D], BF16)
    v_tm = sb("v_tm", [128, D], BF16)
    vgw = sb("vgw", [128, D], BF16)
    reads_bf = sb("reads_bf", [128, D], BF16)
    rkgp = sb("rkgp", [128, D], BF16)
    smt = sb("smt", [128, 128], BF16)
    silu_t = sb("silu_t", [128, 2 * TILE], BF16)
    xtile = overlay("xtile", [128, BPT * D], F32, 0)
    h_sb = overlay("h_sb", [128, NC_H * TILE], BF16, 16384)
    h2_sb = overlay("h2_sb", [128, NC_H * TILE], BF16, 49152)
    hbuf = overlay("hbuf", [128, 2 * DH], BF16, 81920)
    w12buf = overlay("w12buf", [128, 4 * D], BF16, 98304)
    w3buf = overlay("w3buf", [128, 4 * D], BF16, 106496)

    def mv_bnd(e, dk=c_dK.v):
        e.wait_ge(dK, dk)
        e.dma_start(lnb[127:128, D:2 * D], bnd_bf[0:1, :]).then_inc(dK, 16)
        for jc in range(NC_D):
            e.dma_start(ln1fm[:, jc * 129: jc * 129 + 1],
                        bnd_bf[0:1, jc * 128:(jc + 1) * 128]).then_inc(dK, 16)
        e.wait_ge(dK, dk + 16 * (1 + NC_D))
    plan["gpsimd"].append(mv_bnd)
    mv_cnt = c_dK.add(16 * (1 + NC_D))

    state_upd_v = [0] * NC_D
    prev_dmm = [0, 0]
    prev_resid_v = 0
    prev_ln2t_t = 0
    prev_lncast_s = 0
    prev_wkshift = c_dK.v
    prev_ffn_vec = 0
    pend_tr2 = None
    next_ln1 = None
    last_ln2ev = 0
    prev_store_cnt = 0

    prev_res_subs = [0] * BPT
    for t in range(N_TILE):

        def tl_x(e, t=t, prs=tuple(prev_res_subs),
                 lnl=(ln_ready[NPT - 1] if t == 0 else 0)):
            if lnl:
                e.wait_ge(vS, lnl)
            for b in range(BPT):
                g = t * BPT + b
                if prs[b]:
                    e.wait_ge(vS, prs[b])
                e.dma_start(xtile[:, b * D:(b + 1) * D],
                            xs.ap()[g * 128:(g + 1) * 128, :]
                            ).then_inc(dXPs[b], 16)
        plan["sync"].append(tl_x)
        x_sub = [c_dXPs[b].add(16) for b in range(BPT)]

        for b in range(BPT):
            g = t * BPT + b
            par = g % 2

            if b == 0:
                pw_v = [(dK, prev_wkshift)]
                if prev_lncast_s:
                    pw_v.append((sS, prev_lncast_s))
                ln1_v = emit_ln(plan, xtile[:, b * D:(b + 1) * D],
                                mainln[:, :],
                                pre_waits_scalar=[(dXPs[b], x_sub[b])],
                                pre_waits_vector=pw_v, par=0)

                def at_cast(e, par=par, lv=ln1_v, su=state_upd_v[NC_D - 1]):
                    e.wait_ge(vS, lv)
                    if su:
                        e.wait_ge(vS, su)
                    e.copy(lnb[:, par * D:(par + 1) * D],
                           mainln[:, :]).then_inc(sS, 1)
                plan["scalar"].append(at_cast)
                lncast_s = c_s.add(1)

                def at_eadd(e, b=b, cs=lncast_s):
                    e.wait_ge(sS, cs)
                    e.tensor_add(xtile[:, b * D:(b + 1) * D],
                                 xtile[:, b * D:(b + 1) * D],
                                 mainln[:, :]).then_inc(vS, 1)
                plan["vector"].append(at_eadd)
                c_v.add(1)
            else:
                ln1_v, lncast_s = next_ln1
            prev_lncast_s = lncast_s

            def at_tr(e, par=par, cs=lncast_s,
                      pf=(prev_ffn_vec if b == 0 else 0), le=last_ln2ev):
                e.wait_ge(sS, cs)
                if pf:
                    e.wait_ge(vS, pf)
                if le:
                    e.wait_ge(vS, le)
                last = None
                for jc in range(NC_D):
                    last = e.matmul(
                        p01[:, jc * 128:(jc + 1) * 128],
                        lnb[:, par * D + jc * 128: par * D + (jc + 1) * 128],
                        identb[:, :], start=True, stop=True)
                last.then_inc(tS, 1)
            plan["tensor"].append(at_tr)
            tr_done = c_t.add(1)

            def at_ev(e, td=tr_done, g=g):
                e.wait_ge(tS, td)
                if g > 0:
                    for jc in range(NC_D):
                        e.copy(ln1fm[:, jc * 129: jc * 129 + 1],
                               ln1fm[:, jc * 129 + 128: jc * 129 + 129])
                last = None
                for jc in range(NC_D):
                    last = e.copy(ln1fm[:, jc * 129 + 1: jc * 129 + 129],
                                  p01[:, jc * 128:(jc + 1) * 128])
                last.then_inc(sS, 1)
            plan["scalar"].append(at_ev)
            ev_done = c_s.add(1)

            def at_wk(e, par=par, cs=lncast_s, su=state_upd_v[NC_D - 1]):
                e.wait_ge(sS, cs)
                if su:
                    e.wait_ge(vS, su)
                e.dma_start(wk_tm[1:128, :],
                            lnb[0:127, par * D:(par + 1) * D]).then_inc(dK, 16)
                e.dma_start(wk_tm[0:1, :],
                            lnb[127:128, (1 - par) * D:(2 - par) * D]
                            ).then_inc(dK, 16)
            plan["gpsimd"].append(at_wk)
            wk_done = c_dK.add(32)
            prev_wkshift = wk_done

            def at_v(e, ed=ev_done, dww=dw_wwt,
                     wiv=(winit_ev[-1] if g == 0 else 0),
                     mvc=(mv_cnt if g == 0 else 0),
                     pv=(prev_ffn_vec if b == 0 else prev_resid_v)):
                e.wait_ge(sS, ed)
                e.wait_ge(dW0, dww)
                if wiv:
                    e.wait_ge(vS, wiv)
                if mvc:
                    e.wait_ge(dK, mvc)
                if pv:
                    e.wait_ge(vS, pv)
                last = None
                for jc in range(NC_D):
                    for hf in range(2):
                        last = e.matmul(
                            p2[:, hf * 512:(hf + 1) * 512],
                            ln1fm[:, jc * 129 + 1: jc * 129 + 129],
                            wwt_sb[:, jc * D + hf * 512: jc * D + (hf + 1) * 512],
                            start=(jc == 0), stop=(jc == NC_D - 1))
                last.then_inc(tS, 1)
            plan["tensor"].append(at_v)
            v_mm = c_t.add(1)

            def at_s(e, su=state_upd_v[NC_D - 2],
                     mvc=(mv_cnt if g == 0 else 0),
                     pf=(prev_ffn_vec if b == 0 else 0)):
                if su:
                    e.wait_ge(vS, su)
                if mvc:
                    e.wait_ge(dK, mvc)
                if pf:
                    e.wait_ge(vS, pf)
                last = None
                for jc in range(NC_D):
                    last = e.matmul(p4[:, 0:128],
                                    ln1fm[:, jc * 129: jc * 129 + 128],
                                    ln1fm[:, jc * 129 + 1: jc * 129 + 129],
                                    start=(jc == 0), stop=(jc == NC_D - 1))
                last.then_inc(tS, 1)
            plan["tensor"].append(at_s)
            s_mm = c_t.add(1)

            if pend_tr2 is not None:
                def at_tr2d(e, lv=pend_tr2[0], ed=ev_done):
                    e.wait_ge(vS, lv)
                    e.wait_ge(sS, ed)
                    last = None
                    for jc in range(NC_D):
                        last = e.matmul(
                            p01[:, jc * 128:(jc + 1) * 128],
                            ln2b[:, jc * 128:(jc + 1) * 128],
                            identb[:, :], start=True, stop=True)
                    last.then_inc(tS, 1)
                plan["tensor"].append(at_tr2d)
                tr2d_done = c_t.add(1)
                prev_ln2t_t = tr2d_done

            def at_vec1(e, vm=v_mm, sm=s_mm):
                e.wait_ge(tS, vm)
                e.tensor_copy(v_tm[:, :], p2[:, :])
                e.tensor_scalar_mul(vgw[:, :], v_tm[:, :], gw_col[:, 0:1])
                e.wait_ge(tS, sm)
                e.tensor_mul(smt[:, :], p4[:, 0:128], maskT[:, :])
                ins = None
                for jc in range(NC_D):
                    ins = e.tensor_mul(rkgp[:, jc * 128:(jc + 1) * 128],
                                       ln1fm[:, jc * 129 + 1: jc * 129 + 129],
                                       gp_bc[:, :])
                ins.then_inc(vS, 1)
            plan["vector"].append(at_vec1)
            vec1 = c_v.add(1)

            if pend_tr2 is not None:
                def at_ev2d(e, td=tr2d_done, bb=b - 1):
                    e.wait_ge(tS, td)
                    last = None
                    for jc in range(NC_D):
                        last = e.tensor_copy(
                            ln2fm[:, jc * TILE + bb * 128:
                                  jc * TILE + (bb + 1) * 128],
                            p01[:, jc * 128:(jc + 1) * 128])
                    last.then_inc(vS, 1)
                plan["vector"].append(at_ev2d)
                last_ln2ev = c_v.add(1)
                pend_tr2 = None

            if b + 1 < BPT:
                ln1n_v = emit_ln(plan, xtile[:, (b + 1) * D:(b + 2) * D],
                                 mainln[:, :],
                                 pre_waits_scalar=[(dXPs[b + 1], x_sub[b + 1])],
                                 pre_waits_vector=[(sS, lncast_s),
                                                   (dK, wk_done)], par=0)

                def at_castn(e, par=(g + 1) % 2, lv=ln1n_v,
                             su=state_upd_v[NC_D - 1], wd=wk_done):
                    e.wait_ge(vS, lv)
                    if su:
                        e.wait_ge(vS, su)
                    e.wait_ge(dK, wd)
                    e.copy(lnb[:, par * D:(par + 1) * D],
                           mainln[:, :]).then_inc(sS, 1)
                plan["scalar"].append(at_castn)
                next_ln1 = (ln1n_v, c_s.add(1))
                prev_lncast_s = next_ln1[1]

            def at_reads(e, v1=vec1):
                e.wait_ge(vS, v1)
                last = None
                for ic in range(NC_D):
                    for jc in range(NC_D):
                        last = e.matmul(
                            p3[:, ic * 128:(ic + 1) * 128],
                            wstate[:, jc * D + ic * 128: jc * D + (ic + 1) * 128],
                            rkgp[:, jc * 128:(jc + 1) * 128],
                            start=(jc == 0), stop=False)
                    last = e.matmul(p3[:, ic * 128:(ic + 1) * 128],
                                    v_tm[:, ic * 128:(ic + 1) * 128],
                                    smt[:, :], start=False, stop=True)
                last.then_inc(tS, 1)
            plan["tensor"].append(at_reads)
            reads_mm = c_t.add(1)

            def at_rev(e, rm=reads_mm):
                e.wait_ge(tS, rm)
                e.tensor_copy(reads_bf[:, :], p3[:, :]).then_inc(vS, 1)
            plan["vector"].append(at_rev)
            rev_done = c_v.add(1)

            def at_ro(e, rv=rev_done):
                e.wait_ge(vS, rv)
                last = None
                for ic in range(NC_D):
                    for hf in range(2):
                        last = e.matmul(
                            p2[:, hf * 512:(hf + 1) * 512],
                            reads_bf[:, ic * 128:(ic + 1) * 128],
                            wrt_sb[:, ic * D + hf * 512: ic * D + (hf + 1) * 512],
                            start=(ic == 0), stop=(ic == NC_D - 1))
                last.then_inc(tS, 1)
            plan["tensor"].append(at_ro)
            ro_mm = c_t.add(1)

            def at_res(e, rm=ro_mm, b=b, par=par):
                e.wait_ge(tS, rm)
                e.tensor_add(xtile[:, b * D:(b + 1) * D],
                             xtile[:, b * D:(b + 1) * D],
                             p2[:, :]).then_inc(vS, 1)
            plan["vector"].append(at_res)
            prev_resid_v = c_v.add(1)

            if b + 1 < BPT:
                def at_eaddn(e, b=b + 1, cs=next_ln1[1]):
                    e.wait_ge(sS, cs)
                    e.tensor_add(xtile[:, b * D:(b + 1) * D],
                                 xtile[:, b * D:(b + 1) * D],
                                 mainln[:, :]).then_inc(vS, 1)
                plan["vector"].append(at_eaddn)
                c_v.add(1)

            upd_ev_pred = [c_v.v + 1 + jc for jc in range(NC_D)]

            def at_upd(e, wd=wk_done, v1=vec1, rv=rev_done,
                       uep=tuple(upd_ev_pred)):
                e.wait_ge(dK, wd)
                e.wait_ge(vS, v1)
                for jc in range(NC_D):
                    pb = [p4, p3][jc % 2]
                    if jc == 1:
                        e.wait_ge(vS, rv)
                    if jc >= 2:
                        e.wait_ge(vS, uep[jc - 2])
                    last = None
                    for hf in range(2):
                        last = e.matmul(pb[:, hf * 512:(hf + 1) * 512],
                                        wk_tm[:, jc * 128:(jc + 1) * 128],
                                        vgw[:, hf * 512:(hf + 1) * 512],
                                        start=True, stop=True)
                    last.then_inc(tS, 1)
            plan["tensor"].append(at_upd)
            upd_mm = [c_t.add(1) for _ in range(NC_D)]

            def at_updv(e, um=tuple(upd_mm)):
                for jc in range(NC_D):
                    pb = [p4, p3][jc % 2]
                    e.wait_ge(tS, um[jc])
                    e.scalar_tensor_tensor(wstate[:, jc * D:(jc + 1) * D],
                                           wstate[:, jc * D:(jc + 1) * D],
                                           gc_col[:, 0:1], pb[:, :],
                                           OP.mult, OP.add).then_inc(vS, 1)
            plan["vector"].append(at_updv)
            upd_ev = [c_v.add(1) for _ in range(NC_D)]
            assert upd_ev == upd_ev_pred
            state_upd_v = upd_ev

            pw_v2 = [(vS, prev_resid_v)]
            if prev_ln2t_t:
                pw_v2.append((tS, prev_ln2t_t))
            ln2_v = emit_ln(plan, xtile[:, b * D:(b + 1) * D],
                            ln2b[:, :],
                            pre_waits_scalar=[(vS, prev_resid_v)],
                            pre_waits_vector=pw_v2, par=1)

            if b + 1 < BPT:
                pend_tr2 = (ln2_v,)
            else:
                def at_tr2(e, par=par, lv=ln2_v, ed=ev_done, le=last_ln2ev):
                    e.wait_ge(vS, lv)
                    e.wait_ge(sS, ed)
                    if le:
                        e.wait_ge(vS, le)
                    last = None
                    for jc in range(NC_D):
                        last = e.matmul(
                            p01[:, jc * 128:(jc + 1) * 128],
                            ln2b[:, jc * 128:(jc + 1) * 128],
                            identb[:, :], start=True, stop=True)
                    last.then_inc(tS, 1)
                plan["tensor"].append(at_tr2)
                tr2_done = c_t.add(1)
                prev_ln2t_t = tr2_done

                def at_ev2(e, td=tr2_done, b=b):
                    e.wait_ge(tS, td)
                    last = None
                    for jc in range(NC_D):
                        last = e.tensor_copy(
                            ln2fm[:, jc * TILE + b * 128:
                                  jc * TILE + (b + 1) * 128],
                            p01[:, jc * 128:(jc + 1) * 128])
                    last.then_inc(vS, 1)
                plan["vector"].append(at_ev2)
                ln2ev = c_v.add(1)
                last_ln2ev = ln2ev

        ln2_ready = ln2ev

        wmm_done, h_done = [], []
        for oc in range(NC_H):
            par2 = oc % 2
            dw, cdw = dFs[par2], c_dFs[par2]

            if t == 0:
                dw, cdw = dWs[par2], c_dWs[par2]

                def fa_dma(e, oc=oc, par2=par2, dw=dw,
                           wm=(wmm_done[oc - 2] if oc >= 2 else wi_done),
                           sc=(c_dSCs[par2].v if oc >= 2 else 0)):
                    if wm:
                        e.wait_ge(tS, wm)
                    if sc:
                        e.wait_ge(dSCs[par2], sc)
                    e.dma_start(w12buf[:, par2 * 2 * D:(par2 + 1) * 2 * D],
                                w12.ap()[oc]).then_inc(dw, 16)
                plan["gpsimd"].append(fa_dma)
                dcnt = cdw.add(16)

                def fa_scr(e, oc=oc, par2=par2, dw=dw, dc=cdw.v):
                    e.wait_ge(dw, dc)
                    e.dma_start(sc12.ap()[oc],
                                w12buf[:, par2 * 2 * D:(par2 + 1) * 2 * D]
                                ).then_inc(dSCs[par2], 16)
                plan["sync"].append(fa_scr)
                c_dSCs[par2].add(16)
            else:
                def fa_dma(e, oc=oc, par2=par2, dw=dw,
                           wm=(wmm_done[oc - 2] if oc >= 2 else prev_dmm[oc])):
                    if wm:
                        e.wait_ge(tS, wm)
                    e.dma_start(w12buf[:, par2 * 2 * D:(par2 + 1) * 2 * D],
                                sc12.ap()[oc]).then_inc(dw, 16)
                plan["sync"].append(fa_dma)
                dcnt = cdw.add(16)

            def fa_mm(e, oc=oc, par2=par2, dw=dw, dcnt=dcnt, l2r=ln2_ready,
                      hd=(h_done[oc - 2] if oc >= 2 else 0)):
                e.wait_ge(dw, dcnt)
                if oc == 0:
                    e.wait_ge(vS, l2r)
                if hd:
                    e.wait_ge(vS, hd)
                pm = [p01, p3][par2]
                last = None
                base = par2 * 2 * D
                for jc in range(NC_D):
                    last = e.matmul(
                        pm[:, 0:512],
                        w12buf[:, base + jc * 128: base + (jc + 1) * 128],
                        ln2fm[:, jc * TILE: (jc + 1) * TILE],
                        start=(jc == 0), stop=(jc == NC_D - 1))
                for jc in range(NC_D):
                    last = e.matmul(
                        pm[:, 512:1024],
                        w12buf[:, base + D + jc * 128: base + D + (jc + 1) * 128],
                        ln2fm[:, jc * TILE: (jc + 1) * TILE],
                        start=(jc == 0), stop=(jc == NC_D - 1))
                last.then_inc(tS, 1)
            plan["tensor"].append(fa_mm)
            wmm_done.append(c_t.add(1))

            def fa_silu(e, oc=oc, par2=par2, wm=wmm_done[oc]):
                e.wait_ge(tS, wm)
                pm = [p01, p3][par2]
                e.activation(silu_t[:, par2 * TILE:(par2 + 1) * TILE],
                             pm[:, 0:512], AF.Silu,
                             scale=scl["w1s"][:, oc:oc + 1]).then_inc(sS, 1)
            plan["scalar"].append(fa_silu)
            silu_done = c_s.add(1)

            def fa_h(e, oc=oc, par2=par2, sd=silu_done):
                e.wait_ge(sS, sd)
                pm = [p01, p3][par2]
                e.scalar_tensor_tensor(h_sb[:, oc * TILE:(oc + 1) * TILE],
                                       pm[:, 512:1024],
                                       scl["w2s"][:, oc:oc + 1],
                                       silu_t[:, par2 * TILE:(par2 + 1) * TILE],
                                       OP.mult, OP.mult).then_inc(vS, 1)
            plan["vector"].append(fa_h)
            h_done.append(c_v.add(1))

        bmm_done, gel_done = [], []
        for oc in range(NC_H):
            par2 = oc % 2
            dw, cdw = dFs[par2], c_dFs[par2]

            if t == 0:
                dw, cdw = dWs[par2], c_dWs[par2]

                def fb_dma(e, oc=oc, par2=par2, dw=dw,
                           bm=(bmm_done[oc - 2] if oc >= 2
                               else wmm_done[NC_H - 2 + oc]),
                           sc=(c_dSCs[par2].v if oc >= 2 else 0)):
                    if bm:
                        e.wait_ge(tS, bm)
                    if sc:
                        e.wait_ge(dSCs[par2], sc)
                    e.dma_start(hbuf[:, par2 * DH:(par2 + 1) * DH],
                                h1b.ap()[oc]).then_inc(dw, 16)
                plan["gpsimd"].append(fb_dma)
                bcnt = cdw.add(16)

                def fb_scr(e, oc=oc, par2=par2, dw=dw, dc=cdw.v):
                    e.wait_ge(dw, dc)
                    e.dma_start(sch1.ap()[oc],
                                hbuf[:, par2 * DH:(par2 + 1) * DH]
                                ).then_inc(dSCs[par2], 16)
                plan["sync"].append(fb_scr)
                c_dSCs[par2].add(16)
            else:
                def fb_dma(e, oc=oc, par2=par2, dw=dw,
                           bm=(bmm_done[oc - 2] if oc >= 2
                               else wmm_done[NC_H - 2 + oc])):
                    if bm:
                        e.wait_ge(tS, bm)
                    e.dma_start(hbuf[:, par2 * DH:(par2 + 1) * DH],
                                sch1.ap()[oc]).then_inc(dw, 16)
                plan["sync"].append(fb_dma)
                bcnt = cdw.add(16)

            def fb_mm(e, oc=oc, par2=par2, dw=dw, dcnt=bcnt,
                      hlast=h_done[NC_H - 1],
                      gd=(gel_done[oc - 2] if oc >= 2 else 0)):
                e.wait_ge(dw, dcnt)
                if oc == 0:
                    e.wait_ge(vS, hlast)
                if gd:
                    e.wait_ge(sS, gd)
                pm2 = p2[:, par2 * 512:(par2 + 1) * 512]
                last = None
                for kc in range(NC_H):
                    last = e.matmul(
                        pm2,
                        hbuf[:, par2 * DH + kc * 128: par2 * DH + (kc + 1) * 128],
                        h_sb[:, kc * TILE:(kc + 1) * TILE],
                        start=(kc == 0), stop=(kc == NC_H - 1))
                last.then_inc(tS, 1)
            plan["tensor"].append(fb_mm)
            bmm_done.append(c_t.add(1))

            def fb_gelu(e, oc=oc, par2=par2, bm=bmm_done[oc]):
                e.wait_ge(tS, bm)
                pm2 = p2[:, par2 * 512:(par2 + 1) * 512]
                e.activation(h2_sb[:, oc * TILE:(oc + 1) * TILE], pm2, AF.Gelu,
                             scale=scl["h1s"][:, oc:oc + 1]).then_inc(sS, 1)
            plan["scalar"].append(fb_gelu)
            gel_done.append(c_s.add(1))

        cmm_done, gel2_done = [], []
        for oc in range(NC_H):
            par2 = oc % 2
            dw, cdw = dWs[par2], c_dWs[par2]

            def fc_dma(e, oc=oc, par2=par2, dw=dw,
                       cm=(cmm_done[oc - 2] if oc >= 2 else 0),
                       bm=(bmm_done[NC_H - 2 + oc] if oc < 2 else 0)):
                if cm:
                    e.wait_ge(tS, cm)
                if bm:
                    e.wait_ge(tS, bm)
                e.dma_start(hbuf[:, par2 * DH:(par2 + 1) * DH],
                            h2b.ap()[oc]).then_inc(dw, 16)
            plan["gpsimd"].append(fc_dma)
            ccnt = cdw.add(16)

            def fc_mm(e, oc=oc, par2=par2, dw=dw, dcnt=ccnt,
                      glast=gel_done[NC_H - 1],
                      g2=(gel2_done[oc - 2] if oc >= 2 else 0)):
                e.wait_ge(dw, dcnt)
                if oc == 0:
                    e.wait_ge(sS, glast)
                if g2:
                    e.wait_ge(sS, g2)
                pm2 = p2[:, par2 * 512:(par2 + 1) * 512]
                last = None
                for kc in range(NC_H):
                    last = e.matmul(
                        pm2,
                        hbuf[:, par2 * DH + kc * 128: par2 * DH + (kc + 1) * 128],
                        h2_sb[:, kc * TILE:(kc + 1) * TILE],
                        start=(kc == 0), stop=(kc == NC_H - 1))
                last.then_inc(tS, 1)
            plan["tensor"].append(fc_mm)
            cmm_done.append(c_t.add(1))

            def fc_gelu(e, oc=oc, par2=par2, cm=cmm_done[oc]):
                e.wait_ge(tS, cm)
                pm2 = p2[:, par2 * 512:(par2 + 1) * 512]
                e.activation(h_sb[:, oc * TILE:(oc + 1) * TILE], pm2, AF.Gelu,
                             scale=scl["h2s"][:, oc:oc + 1]).then_inc(sS, 1)
            plan["scalar"].append(fc_gelu)
            gel2_done.append(c_s.add(1))

        psO = [p01, p2, p3, p4]
        dmm_done = []
        for kp in range(NC_H // 2):
            par2 = kp % 2
            dw, cdw = dFs[par2], c_dFs[par2]

            if t == 0:
                dw, cdw = dWs[par2], c_dWs[par2]

                def fd_dma(e, kp=kp, par2=par2, dw=dw,
                           dm=(dmm_done[kp - 2] if kp >= 2 else
                               cmm_done[NC_H - 2 + kp] if kp < 2 else 0),
                           sc=c_dSCs[par2].v):
                    if dm:
                        e.wait_ge(tS, dm)
                    e.wait_ge(dSCs[par2], sc)
                    e.dma_start(w3buf[:, par2 * 2 * D:(par2 + 1) * 2 * D],
                                w3r.ap()[kp]).then_inc(dw, 16)
                plan["gpsimd"].append(fd_dma)
                dcnt3 = cdw.add(16)

                def fd_scr(e, kp=kp, par2=par2, dw=dw, dc=cdw.v):
                    e.wait_ge(dw, dc)
                    e.dma_start(sc3.ap()[kp],
                                w3buf[:, par2 * 2 * D:(par2 + 1) * 2 * D]
                                ).then_inc(dSCs[par2], 16)
                plan["sync"].append(fd_scr)
                c_dSCs[par2].add(16)
            else:
                def fd_dma(e, kp=kp, par2=par2, dw=dw,
                           dm=(dmm_done[kp - 2] if kp >= 2 else
                               cmm_done[NC_H - 2 + kp] if kp < 2 else 0)):
                    if dm:
                        e.wait_ge(tS, dm)
                    e.dma_start(w3buf[:, par2 * 2 * D:(par2 + 1) * 2 * D],
                                sc3.ap()[kp]).then_inc(dw, 16)
                plan["sync"].append(fd_dma)
                dcnt3 = cdw.add(16)

            def fd_mm(e, kp=kp, par2=par2, dw=dw, dcnt=dcnt3,
                      g2last=gel2_done[NC_H - 1]):
                e.wait_ge(dw, dcnt)
                if kp == 0:
                    e.wait_ge(sS, g2last)
                last = None
                for j in range(2):
                    kc = 2 * kp + j
                    for sub in range(BPT):
                        for hf in range(2):
                            last = e.matmul(
                                psO[sub][:, hf * 512:(hf + 1) * 512],
                                h_sb[:, kc * TILE + sub * 128:
                                     kc * TILE + (sub + 1) * 128],
                                w3buf[:, par2 * 2 * D + j * D + hf * 512:
                                      par2 * 2 * D + j * D + (hf + 1) * 512],
                                start=(kc == 0), stop=(kc == NC_H - 1))
                last.then_inc(tS, 1)
            plan["tensor"].append(fd_mm)
            dmm_done.append(c_t.add(1))

        def fd_res(e, dm=dmm_done[NC_H // 2 - 1]):
            e.wait_ge(tS, dm)
            for sub in range(BPT):
                e.tensor_mul(psO[sub][:, :], psO[sub][:, :], s3bc[:, :])
                e.tensor_add(xtile[:, sub * D:(sub + 1) * D],
                             xtile[:, sub * D:(sub + 1) * D],
                             psO[sub][:, :]).then_inc(vS, 1)
        plan["vector"].append(fd_res)
        res_done = [c_v.add(1) for _ in range(BPT)]
        prev_ffn_vec = res_done[-1]
        prev_res_subs = res_done

        def fd_store(e, t=t, rd=tuple(res_done)):
            for sub in range(BPT):
                g = t * BPT + sub
                e.wait_ge(vS, rd[sub])
                e.dma_start(o_seg.ap()[g * 128:(g + 1) * 128, :],
                            xtile[:, sub * D:(sub + 1) * D]).then_inc(dO, 16)
        plan["sync"].append(fd_store)
        prev_store_cnt = c_dO.add(16 * BPT)
        prev_dmm = [dmm_done[NC_H // 2 - 2], dmm_done[NC_H // 2 - 1]]

    def fin(e, total=c_dO.v):
        e.wait_ge(dO, total)
    plan["sync"].append(fin)
    run_block(plan)

    es.close()
    return nc


_NC_CACHE = None


def prep_in_maps(L0_attn_out, L0_attn_W_write, L0_attn_W_read, L0_attn_decay,
                 L0_attn_log_alpha, L0_ffn_w1_w, L0_ffn_w1_s, L0_ffn_w2_w,
                 L0_ffn_w2_s, L0_ffn_h1_w, L0_ffn_h1_s, L0_ffn_h2_w,
                 L0_ffn_h2_s, L0_ffn_w3_w, L0_ffn_w3_s):
    x = np.ascontiguousarray(np.asarray(L0_attn_out, dtype=np.float32))
    B, T, _ = x.shape

    cont = np.ascontiguousarray
    wwt = cont(np.asarray(L0_attn_W_write, np.float32).T)
    wrt = cont(np.asarray(L0_attn_W_read, np.float32).T)
    w1b = (np.asarray(L0_ffn_w1_w, np.float32)
           .reshape(NC_H, 128, NC_D, 128).transpose(0, 3, 2, 1)
           .reshape(NC_H, 128, NC_D * 128))
    w2b = (np.asarray(L0_ffn_w2_w, np.float32)
           .reshape(NC_H, 128, NC_D, 128).transpose(0, 3, 2, 1)
           .reshape(NC_H, 128, NC_D * 128))
    w12 = cont(np.concatenate([w1b, w2b], axis=2))
    h1b = cont(np.asarray(L0_ffn_h1_w, np.float32)
               .reshape(NC_H, 128, NC_H, 128).transpose(0, 3, 2, 1)
               .reshape(NC_H, 128, NC_H * 128))
    h2b = cont(np.asarray(L0_ffn_h2_w, np.float32)
               .reshape(NC_H, 128, NC_H, 128).transpose(0, 3, 2, 1)
               .reshape(NC_H, 128, NC_H * 128))
    w3r = cont(np.asarray(L0_ffn_w3_w, np.float32).T
               .reshape(NC_H // 2, 2, 128, D).transpose(0, 2, 1, 3)
               .reshape(NC_H // 2, 128, 2 * D))
    w1s = cont(np.asarray(L0_ffn_w1_s, np.float32).reshape(NC_H, 128).T)
    w2s = cont(np.asarray(L0_ffn_w2_s, np.float32).reshape(NC_H, 128).T)
    h1s = cont(np.asarray(L0_ffn_h1_s, np.float32).reshape(NC_H, 128).T)
    h2s = cont(np.asarray(L0_ffn_h2_s, np.float32).reshape(NC_H, 128).T)
    s3bc = cont(np.broadcast_to(np.asarray(L0_ffn_w3_s, np.float32)[None, :],
                                (128, D)))
    dec = np.full((128, 1), np.float32(np.asarray(L0_attn_decay).reshape(())),
                  dtype=np.float32)
    la = np.full((128, 1), np.float32(np.asarray(L0_attn_log_alpha).reshape(())),
                 dtype=np.float32)

    in_maps = []
    for core in range(8):
        b, s = core // 4, core % 4
        seg0 = s * S_LEN
        xs_i = cont(x[b, seg0:seg0 + S_LEN])
        xp_i = np.zeros((P_LEN, D), np.float32)
        n_real = min(seg0, P_LEN)
        if n_real > 0:
            xp_i[P_LEN - n_real:] = x[b, seg0 - n_real:seg0]
        in_maps.append({
            "xs": xs_i, "xp": xp_i, "wwt": wwt, "wrt": wrt,
            "w12": w12, "h1b": h1b, "h2b": h2b, "w3r": w3r,
            "w1s": w1s, "w2s": w2s, "h1s": h1s, "h2s": h2s, "s3bc": s3bc,
            "dec": dec, "la": la,
        })
    return in_maps


def kernel(**inputs):
    global _NC_CACHE
    if _NC_CACHE is None:
        _NC_CACHE = build_nc()
    nc = _NC_CACHE
    in_maps = prep_in_maps(**inputs)
    res = run_bass_kernel_spmd(nc, in_maps, core_ids=list(range(8)))
    B = 2
    out = np.empty((B, B * 4 * S_LEN // B, D), np.float32)
    for core in range(8):
        b, s = core // 4, core % 4
        out[b, s * S_LEN:(s + 1) * S_LEN] = res.results[core]["o_seg"]
    return out
